# version 21
# baseline (speedup 1.0000x reference)
import numpy as np
import ml_dtypes
import concourse.bass as bass
import concourse.mybir as mybir
from concourse.bass_utils import run_bass_kernel_spmd

F32 = mybir.dt.float32
BF16 = mybir.dt.bfloat16
I32 = mybir.dt.int32
AF = mybir.ActivationFunctionType
ALU = mybir.AluOpType
BF = ml_dtypes.bfloat16

NCORE = 8
D = 4096
NK = 32
NN = 32
TPC = 2048
TP = 1024
NPASS = 2
TB = 512
EPS_LN = 1e-5
EPS_PRIOR = 1e-12
SCALING = 2.0


class Ev:
    __slots__ = ("sem", "count")

    def __init__(self, sem, count=None):
        self.sem = sem
        self.count = count


class Res:
    __slots__ = ("last_w", "readers", "name")

    def __init__(self, name):
        self.name = name
        self.last_w = None
        self.readers = []


class Op:
    __slots__ = ("eng", "fn", "waits", "inc", "tag")

    def __init__(self, eng, fn, waits, inc, tag):
        self.eng, self.fn, self.waits, self.inc, self.tag = \
            eng, fn, waits, inc, tag


class Builder:
    ENGS = ("sync", "tensor", "vector", "scalar", "gpsimd")
    ENG_SEM = {"tensor": "pe", "vector": "dve", "scalar": "act",
               "gpsimd": "pool"}

    def __init__(self):
        self.ops = {e: [] for e in self.ENGS}
        self.counts = {}
        self.pending = {e: [] for e in self.ENGS}
        self.res = {}
        self.semnames = set()

    def r(self, name):
        if name not in self.res:
            self.res[name] = Res(name)
        return self.res[name]

    def sem(self, name):
        self.semnames.add(name)
        self.counts.setdefault(name, 0)
        return name

    def ev_at(self, semname, count):
        return Ev(self.sem(semname), count)

    def _mkop(self, eng, fn, wl, inc, tag):
        if inc is not None:
            semname, amt = inc
            self.sem(semname)
            self.counts[semname] += amt
            ev = Ev(semname, self.counts[semname])
            if semname == self.ENG_SEM.get(eng):
                for p in self.pending[eng]:
                    p.count = self.counts[semname]
                self.pending[eng] = []
        else:
            ev = Ev(self.sem(self.ENG_SEM[eng]))
            self.pending[eng].append(ev)
        self.ops[eng].append(Op(eng, fn, wl, inc, tag))
        return ev

    def op(self, eng, fn, reads=(), writes=(), waits=(), inc=None, tag=""):
        if inc is None and eng in ("vector", "scalar", "gpsimd"):
            inc = (self.ENG_SEM[eng], 1)
        reads = [self.r(x) if isinstance(x, str) else x for x in reads]
        writes = [self.r(x) if isinstance(x, str) else x for x in writes]
        wl = list(waits)
        for rs in reads:
            if rs.last_w is not None:
                wl.append(rs.last_w)
        for rs in writes:
            if rs.last_w is not None:
                wl.append(rs.last_w)
            wl.extend(rs.readers)
        ev = self._mkop(eng, fn, wl, inc, tag)
        for rs in reads:
            rs.readers.append(ev)
        for rs in writes:
            rs.last_w = ev
            rs.readers = []
        return ev

    def acc_group(self, eng, fns, reads_each, writes, waits=(), inc=None,
                  tag=""):
        writes = [self.r(x) if isinstance(x, str) else x for x in writes]
        wl0 = list(waits)
        for rs in writes:
            if rs.last_w is not None:
                wl0.append(rs.last_w)
            wl0.extend(rs.readers)
        n = len(fns)
        for j, fn in enumerate(fns):
            reads = [self.r(x) if isinstance(x, str) else x
                     for x in reads_each[j]]
            wl = list(wl0) if j == 0 else []
            for rs in reads:
                if rs.last_w is not None:
                    wl.append(rs.last_w)
            ev = self._mkop(eng, fn, wl, inc if j == n - 1 else None,
                            f"{tag}[{j}]")
            for rs in reads:
                rs.readers.append(ev)
        for rs in writes:
            rs.last_w = ev
            rs.readers = []
        return ev

    def finalize(self):
        for eng in ("tensor", "vector", "scalar", "gpsimd"):
            if self.pending[eng]:
                semn = self.sem(self.ENG_SEM[eng])
                self.counts[semn] += 1
                for p in self.pending[eng]:
                    p.count = self.counts[semn]
                self.pending[eng] = []
                self.ops[eng].append(Op(eng, "NOP_INC", [], (semn, 1),
                                        "final_resolve"))

    def validate(self):
        for e in self.ENGS:
            for o in self.ops[e]:
                for w in o.waits:
                    assert w.count is not None, \
                        f"unresolved wait in {e}:{o.tag} (sem {w.sem})"
        ptr = {e: 0 for e in self.ENGS}
        cnt = {s: 0 for s in self.semnames}
        total = sum(len(v) for v in self.ops.values())
        done = 0
        progress = True
        while progress:
            progress = False
            for e in self.ENGS:
                own = self.ENG_SEM.get(e)
                while ptr[e] < len(self.ops[e]):
                    o = self.ops[e][ptr[e]]
                    if any(cnt[w.sem] < w.count for w in o.waits):
                        break
                    if o.inc is not None:
                        cnt[o.inc[0]] += o.inc[1]
                    ptr[e] += 1
                    done += 1
                    progress = True
        if done != total:
            msg = []
            for e in self.ENGS:
                if ptr[e] < len(self.ops[e]):
                    o = self.ops[e][ptr[e]]
                    bad = [(w.sem, w.count, cnt[w.sem]) for w in o.waits
                           if cnt[w.sem] < w.count]
                    msg.append(f"{e}@{ptr[e]}/{len(self.ops[e])} "
                               f"tag={o.tag} missing={bad}")
            raise RuntimeError("schedule deadlock:\n" + "\n".join(msg))


def build_program():
    nc = bass.Bass("TRN2", target_bir_lowering=False)
    B = Builder()

    d_xt = nc.dram_tensor("xt", [D, TPC], F32, kind="ExternalInput").ap()
    d_wp = nc.dram_tensor("wp", [NN, 128, D], F32, kind="ExternalInput").ap()
    d_wbp = nc.dram_tensor("wbp", [NN, 128, D], F32, kind="ExternalInput").ap()
    d_mk = nc.dram_tensor("mkp", [D, D], I32, kind="ExternalInput").ap()
    d_ap = nc.dram_tensor("ap_", [128, D], F32, kind="ExternalInput").ap()
    d_rp = nc.dram_tensor("rp", [128, 512], F32, kind="ExternalInput").ap()
    d_bp = nc.dram_tensor("bp", [128, D], F32, kind="ExternalInput").ap()
    d_bb = nc.dram_tensor("bb", [32, 128], F32, kind="ExternalInput").ap()
    d_rbv = nc.dram_tensor("rbv", [16, 1], F32, kind="ExternalInput").ap()
    d_pri = nc.dram_tensor("pri", [16, 1], F32, kind="ExternalInput").ap()
    d_act = nc.dram_tensor("actv", [1, 4], I32, kind="ExternalInput").ap()
    d_ident = nc.dram_tensor("ident", [32, 32], F32, kind="ExternalInput").ap()
    d_ones1 = nc.dram_tensor("ones1f", [1, 128], F32, kind="ExternalInput").ap()
    d_onesb = nc.dram_tensor("onesbf", [128, 1], BF16, kind="ExternalInput").ap()
    d_e4l = nc.dram_tensor("e4l", [4, 128], BF16, kind="ExternalInput").ap()
    d_e4p = nc.dram_tensor("e4p", [128, 4], BF16, kind="ExternalInput").ap()
    d_ekron = nc.dram_tensor("ekron", [16, 128], F32, kind="ExternalInput").ap()
    d_iota = nc.dram_tensor("iota16", [1, 16], F32, kind="ExternalInput").ap()
    d_ones4 = nc.dram_tensor("ones4f", [4, 1], F32, kind="ExternalInput").ap()
    d_out = nc.dram_tensor("out", [D, TPC], F32, kind="ExternalOutput").ap()
    d_scr2 = nc.dram_tensor("scr2", [128, 128], BF16, kind="Internal").ap()
    from contextlib import ExitStack
    ctx = ExitStack()
    E = ctx.enter_context

    xn = E(nc.sbuf_tensor("xn", [128, NK * TP], BF16))
    wstage = [E(nc.sbuf_tensor(f"wstage{j}", [128, 1024], F32)) for j in range(2)]
    wbstage = [E(nc.sbuf_tensor(f"wbstage{j}", [128, 1024], F32)) for j in range(2)]
    mstage = [E(nc.sbuf_tensor(f"mstage{j}", [128, 1024], I32)) for j in range(3)]
    wm = [E(nc.sbuf_tensor(f"wm{j}", [128, D], BF16)) for j in range(2)]
    wb = [E(nc.sbuf_tensor(f"wb{j}", [128, D], BF16)) for j in range(2)]
    mcB = E(nc.sbuf_tensor("mcB", [128, 512], BF16))
    mcC = E(nc.sbuf_tensor("mcC", [128, 256], BF16))
    pooled = E(nc.sbuf_tensor("pooled", [128, 128], BF16))
    bm01a = E(nc.sbuf_tensor("bm01a", [128, 128], BF16))
    sgn_t = E(nc.sbuf_tensor("sgn_t", [128, 4], F32))
    sgn_t2 = E(nc.sbuf_tensor("sgn_t2", [128, 128], BF16))
    bmra = E(nc.sbuf_tensor("bmra", [4, 2048], BF16))
    bmtexp = E(nc.sbuf_tensor("bmtexp", [128, D], BF16))
    xbuf = [E(nc.sbuf_tensor(f"xbuf{j}", [128, TP], F32)) for j in range(2)]
    out_sb = [E(nc.sbuf_tensor(f"out_sb{j}", [128, TP], F32)) for j in range(2)]
    lA = E(nc.sbuf_tensor("lA", [128, D], BF16))
    lR = E(nc.sbuf_tensor("lR", [128, 512], BF16))
    lB = E(nc.sbuf_tensor("lB", [128, D], BF16))
    z_sb = E(nc.sbuf_tensor("z_sb", [128, TP], BF16))
    zw = E(nc.sbuf_tensor("zw", [128, TP], BF16))
    txn = E(nc.sbuf_tensor("txn", [128, TP], BF16))
    sq = E(nc.sbuf_tensor("sq", [128, 2048], BF16))
    aexp2 = [E(nc.sbuf_tensor(f"aexp2_{j}", [128, TB], BF16)) for j in range(2)]
    rs_bf = [E(nc.sbuf_tensor(f"rs_bf{j}", [128, TB], BF16)) for j in range(2)]
    _lg = E(nc.sbuf_tensor("lg_tmp", [16, TB], F32))
    lg_tmp2 = [_lg, _lg]
    esb2 = [E(nc.sbuf_tensor(f"esb{j}", [4, TB], F32)) for j in range(2)]
    srow = E(nc.sbuf_tensor("srow", [1, TB], F32))
    qrow = E(nc.sbuf_tensor("qrow", [1, TB], F32))
    g_sb = E(nc.sbuf_tensor("g_sb", [128, TP], F32))
    sg_sb = E(nc.sbuf_tensor("sg_sb", [128, TP], F32))
    b_sb = E(nc.sbuf_tensor("b_sb", [128, 32], F32))
    bb_sb = E(nc.sbuf_tensor("bb_sb", [32, 128], F32))
    ident = E(nc.sbuf_tensor("ident_sb", [32, 32], F32))
    ones1 = E(nc.sbuf_tensor("ones1_sb", [1, 128], F32))
    onesb = E(nc.sbuf_tensor("onesb_sb", [128, 1], BF16))
    e4l = E(nc.sbuf_tensor("e4l_sb", [4, 128], BF16))
    e4p = E(nc.sbuf_tensor("e4p_sb", [128, 4], BF16))
    ekron = E(nc.sbuf_tensor("ekron_sb", [16, 128], F32))
    iota = E(nc.sbuf_tensor("iota_sb", [1, 16], F32))
    ones4 = E(nc.sbuf_tensor("ones4_sb", [4, 1], F32))
    actv = E(nc.sbuf_tensor("actv_sb", [1, 4], I32))
    rbv = E(nc.sbuf_tensor("rbv_sb", [16, 1], F32))
    pri = E(nc.sbuf_tensor("pri_sb", [16, 1], F32))
    a_f = E(nc.sbuf_tensor("a_f", [1, 4], F32))
    a_cl = E(nc.sbuf_tensor("a_cl", [1, 4], F32))
    a_col = E(nc.sbuf_tensor("a_col", [4, 1], F32))
    dt_sb = E(nc.sbuf_tensor("dt_sb", [4, 16], F32))
    ab_sb = E(nc.sbuf_tensor("ab_sb", [4, 16], F32))
    s_sb = E(nc.sbuf_tensor("s_sb", [4, 16], F32))
    st_sb = E(nc.sbuf_tensor("st_sb", [16, 4], F32))
    rbse = E(nc.sbuf_tensor("rbse", [4, 1], F32))
    rbse2 = E(nc.sbuf_tensor("rbse2", [4, 1], F32))
    lp_sb = E(nc.sbuf_tensor("lp_sb", [4, 1], F32))
    ce_sb = E(nc.sbuf_tensor("ce_sb", [4, 1], F32))
    m2f = E(nc.sbuf_tensor("m2f", [4, 128], F32))
    pb = [E(nc.psum_tensor(f"pb{j}", [128, 512], F32)) for j in range(8)]

    lnA = g_sb[:, 0:512]
    lnB = g_sb[:, 512:1024]
    lnC = sg_sb[:, 0:512]
    lnD = sg_sb[:, 512:1024]
    rb_v = [sq[:, 0:512], sq[:, 512:1024]]
    mrb_v = [sq[:, 1024:1536], sq[:, 1536:2048]]
    rb_all = sq[:, 0:1024]
    mrb_all = sq[:, 1024:2048]

    def dma(semn, dst, src, reads=(), writes=(), waits=(), tag=""):
        return B.op("sync", (lambda e, d=dst, s=src: e.dma_start(out=d, in_=s)),
                    reads=reads, writes=writes, waits=waits,
                    inc=(semn, 16), tag=tag)

    def adma(semn, dst, src, reads=(), writes=(), waits=(), tag=""):
        return B.op("scalar", (lambda e, d=dst, s=src: e.dma_start(out=d, in_=s)),
                    reads=reads, writes=writes, waits=waits,
                    inc=(semn, 16), tag=tag)

    def gdma(semn, dst, src, reads=(), writes=(), waits=(), tag="", slow=False):
        return B.op("gpsimd",
                    (lambda e, d=dst, s=src, sl=slow: e.dma_start(
                        out=d, in_=s, allow_slow_non_contiguous=sl)),
                    reads=reads, writes=writes, waits=waits,
                    inc=(semn, 16), tag=tag)

    def plan_startup():
        for dst, src, nm in [
            (ident[:], d_ident, "ident"), (ones1[:], d_ones1, "ones1"),
            (onesb[:], d_onesb, "onesb"), (e4l[:], d_e4l, "e4l"),
            (e4p[:], d_e4p, "e4p"), (ekron[:], d_ekron, "ekron"),
            (iota[:], d_iota, "iota"), (ones4[:], d_ones4, "ones4"),
            (actv[:], d_act, "actv"), (rbv[:], d_rbv, "rbv"),
            (pri[:], d_pri, "pri"), (bb_sb[:], d_bb, "bb"),
        ]:
            dma("init", dst, src, tag=f"ld_{nm}")
        for q in range(4):
            sl = q % 2
            dma(f"sw{sl}", wstage[sl][:], d_ap[:, q * 1024:(q + 1) * 1024],
                writes=[f"wst{sl}"], tag=f"ld_lA{q}")
            B.op("scalar",
                 lambda e, q=q, sl=sl: e.copy(lA[:, q * 1024:(q + 1) * 1024],
                                              wstage[sl][:]),
                 reads=[f"wst{sl}"], writes=["lA"], tag=f"c_lA{q}")
        for q in range(4):
            sl = q % 2
            dma(f"swb{sl}", wbstage[sl][:], d_bp[:, q * 1024:(q + 1) * 1024],
                writes=[f"wbst{sl}"], tag=f"ld_lB{q}")
            B.op("scalar",
                 lambda e, q=q, sl=sl: e.copy(lB[:, q * 1024:(q + 1) * 1024],
                                              wbstage[sl][:]),
                 reads=[f"wbst{sl}"], writes=["lB"], tag=f"c_lB{q}")
        dma("sw0", wstage[0][:, 0:512], d_rp, writes=["wst0"], tag="ld_lR")
        B.op("scalar", lambda e: e.copy(lR[:], wstage[0][:, 0:512]),
             reads=["wst0"], writes=["lR"], tag="c_lR")

        winit = [B.ev_at("init", B.counts.get("init", 0))]

        B.op("scalar", lambda e: e.copy(a_f[:], actv[:]),
             waits=winit, writes=["a_f"], tag="cast_active")
        B.op("vector",
             lambda e: e.tensor_scalar(a_cl[:], a_f[:], 0.0, 15.0,
                                       ALU.max, ALU.min),
             reads=["a_f"], writes=["a_cl"], tag="clamp_active")
        B.op("tensor",
             lambda e: e.transpose(pb[4][0:4, 0:1], a_cl[:], ident[0:1, 0:1]),
             waits=winit, reads=["a_cl"], writes=["pb4"], inc=("pe", 1),
             tag="tr_acol")
        B.op("vector", lambda e: e.tensor_copy(a_col[:], pb[4][0:4, 0:1]),
             reads=["pb4"], writes=["a_col"], tag="ev_acol")
        B.op("tensor",
             lambda e: e.matmul(pb[5][0:4, 0:16], ones1[0:1, 0:4], iota[:],
                                start=True, stop=True),
             waits=winit, writes=["pb5"], inc=("pe", 1), tag="mm_iotab")
        B.op("vector",
             lambda e: e.tensor_single_scalar(dt_sb[:], pb[5][0:4, 0:16],
                                              a_col[:], ALU.subtract),
             reads=["pb5", "a_col"], writes=["dt"], tag="d_sub")
        B.op("scalar", lambda e: e.activation(ab_sb[:], dt_sb[:], AF.Abs),
             reads=["dt"], writes=["absd"], tag="absd")
        B.op("vector",
             lambda e: e.tensor_scalar(dt_sb[:], ab_sb[:], -1.0, 1.0,
                                       ALU.mult, ALU.add),
             reads=["absd"], writes=["dt"], tag="S_lin")
        B.op("vector",
             lambda e: e.tensor_scalar_max(s_sb[:], dt_sb[:], 0.0),
             reads=["dt"], writes=["S"], tag="S_onehot")
        B.op("tensor",
             lambda e: e.transpose(pb[6][0:16, 0:4], s_sb[:], ident[0:4, 0:4]),
             reads=["S"], writes=["pb6"], inc=("pe", 1), tag="tr_S")
        B.op("vector", lambda e: e.tensor_copy(st_sb[:], pb[6][0:16, 0:4]),
             reads=["pb6"], writes=["st"], tag="ev_st")
        B.op("tensor",
             lambda e: e.matmul(pb[7][0:4, 0:1], st_sb[:], rbv[:],
                                start=True, stop=True),
             waits=winit, reads=["st"], writes=["pb7"], inc=("pe", 1),
             tag="mm_rbsel")
        B.op("tensor",
             lambda e: e.matmul(pb[7][0:4, 1:2], st_sb[:], pri[:],
                                start=True, stop=True),
             reads=["st"], writes=["pb7"], inc=("pe", 1), tag="mm_prisel")
        B.op("vector", lambda e: e.tensor_copy(rbse[:], pb[7][0:4, 0:1]),
             reads=["pb7"], writes=["rbse"], tag="ev_rbse")
        B.op("vector",
             lambda e: e.tensor_scalar_add(rbse2[:], pb[7][0:4, 1:2],
                                           EPS_PRIOR),
             reads=["pb7"], writes=["prieps"], tag="pri_eps")
        B.op("scalar",
             lambda e: e.activation(lp_sb[:], rbse2[:], AF.Ln),
             reads=["prieps"], writes=["lp"], tag="ln_prior")
        B.op("vector", lambda e: e.tensor_add(ce_sb[:], rbse[:], lp_sb[:]),
             reads=["rbse", "lp"], writes=["ce"], tag="ce")
        B.op("tensor",
             lambda e: e.matmul(pb[4][0:4, 0:128], st_sb[:], ekron[:],
                                start=True, stop=True),
             reads=["st"], writes=["pb4"], inc=("pe", 1), tag="mm_M2")
        B.op("vector", lambda e: e.tensor_copy(m2f[:], pb[4][0:4, 0:128]),
             reads=["pb4"], writes=["m2f"], tag="ev_M2")
        B.op("tensor",
             lambda e: e.transpose(pb[5][:, 0:32], bb_sb[:], ident[0:32, 0:32]),
             waits=winit, writes=["pb5"], inc=("pe", 1), tag="tr_bb")
        B.op("vector", lambda e: e.tensor_copy(b_sb[:], pb[5][:, 0:32]),
             reads=["pb5"], writes=["b_sb"], tag="ev_bb")

    def band_mask_half(b, h):
        for q in (2 * h, 2 * h + 1):
            sl = (b * 4 + q) % 3
            adma(f"sm{sl}", mstage[sl][:],
                 d_mk[b * 128:(b + 1) * 128, q * 1024:(q + 1) * 1024],
                 writes=[f"mst{sl}"], tag=f"mkq_b{b}q{q}")
            B.op("vector",
                 lambda e, sl=sl: e.tensor_add(mcB[:], mstage[sl][:, 0:512],
                                               mstage[sl][:, 512:1024]),
                 reads=[f"mst{sl}"], writes=["mcB"], tag=f"h1_b{b}q{q}")
            B.op("vector",
                 lambda e: e.tensor_add(mcC[:], mcB[:, 0:256], mcB[:, 256:512]),
                 reads=["mcB"], writes=["mcC"], tag=f"h2_b{b}q{q}")
            B.op("vector",
                 lambda e: e.tensor_add(mcB[:, 0:128], mcC[:, 0:128],
                                        mcC[:, 128:256]),
                 reads=["mcC"], writes=["mcB"], tag=f"h3_b{b}q{q}")
            B.op("vector",
                 lambda e: e.tensor_add(mcC[:, 0:64], mcB[:, 0:64],
                                        mcB[:, 64:128]),
                 reads=["mcB"], writes=["mcC"], tag=f"h4_b{b}q{q}")
            B.op("vector",
                 lambda e, q=q: e.tensor_add(pooled[:, q * 32:(q + 1) * 32],
                                             mcC[:, 0:32], mcC[:, 32:64]),
                 reads=["mcC"], writes=["pooled"], tag=f"h5_b{b}q{q}")

    def band_load_s2(b):
        band_mask_half(b, 0)
        band_mask_half(b, 1)

    def band_pooledmm(b, bank):
        B.op("tensor",
             lambda e, bank=bank: e.matmul(pb[bank][:, 128:132], pooled[:],
                                           e4p[:], start=True, stop=True),
             reads=["pooled"], writes=[f"pb{bank}"], inc=("pe", 1),
             tag=f"poolmm_b{b}")

    def band_pevac(b, bank):
        B.op("vector",
             lambda e, bank=bank, b=b: e.tensor_copy(
                 bm01a[:, b * 4:(b + 1) * 4], pb[bank][:, 128:132]),
             reads=[f"pb{bank}"], writes=["bm01a"], tag=f"pev_b{b}")

    def bounce_out_all():
        dma("sbb0", d_scr2, bm01a[:], reads=["bm01a"], writes=["scr2"],
            tag="bout_all")

    def bounce_in_half(h):
        dst = bmra.ap().rearrange("j (kc x) -> j kc x", kc=32)
        src = d_scr2.rearrange("(kc j) x -> j kc x", j=4)[:, :, h * 64:(h + 1) * 64]
        dma("sbb0", dst, src, reads=["scr2"], writes=["bmra"],
            tag=f"bin_h{h}")

    def band_e4l(b, bank):
        bl = b % 16

        def fn(e, bank=bank, bl=bl):
            rhs = bass.AP(bmra.ap().tensor, bl * 4, [[2048, 4], [64, 32], [1, 4]])
            out = pb[bank][:, 0:128].rearrange("p (kc nb) -> p kc nb", nb=4)
            return e.matmul(out, e4l[:], rhs, start=True, stop=True)
        B.op("tensor", fn, reads=["bmra"], writes=[f"pb{bank}"],
             inc=("pe", 1), tag=f"e4l_b{b}")

    def band_texp_evac(b, bank):
        B.op("scalar",
             lambda e, bank=bank: e.activation(sgn_t2[:], pb[bank][:, 0:128],
                                               AF.Sign),
             reads=[f"pb{bank}"], writes=["sgn2"], tag=f"tsgn_b{b}")
        B.op("scalar",
             lambda e, b=b: e.activation(bmtexp[:, b * 128:(b + 1) * 128],
                                         sgn_t2[:], AF.Relu),
             reads=["sgn2"], writes=[f"texp{b}"], tag=f"texp_b{b}")

    def w_loads_half(p, i, h):
        for q in (2 * h, 2 * h + 1):
            sl = q % 2
            dma(f"sw{sl}", wstage[sl][:], d_wp[i][:, q * 1024:(q + 1) * 1024],
                writes=[f"wst{sl}"], tag=f"Wq_p{p}i{i}q{q}")
        for q in (2 * h, 2 * h + 1):
            sl = q % 2
            adma(f"swb{sl}", wbstage[sl][:],
                 d_wbp[i][:, q * 1024:(q + 1) * 1024],
                 writes=[f"wbst{sl}"], tag=f"WBq_p{p}i{i}q{q}")

    def wm_mults_half(p, i, h):
        dst = wm[i % 2]
        for q in (2 * h, 2 * h + 1):
            sl = q % 2

            def fn(e, q=q, sl=sl, dst=dst, i=i):
                o = dst[:, q * 1024:(q + 1) * 1024].rearrange(
                    "p (kc a b) -> p kc a b", a=4, b=32)
                i1 = wstage[sl][:].rearrange("p (kc a b) -> p kc a b",
                                             a=4, b=32)
                i2 = bass.AP(bmtexp.ap().tensor, i * 128 + q * 32,
                             [[D, 128], [4, 8], [1, 4], [0, 32]])
                return e.tensor_tensor(o, i1, i2, ALU.mult)
            B.op("vector", fn, reads=[f"wst{sl}", f"texp{i}"],
                 writes=[f"wm{i % 2}"], tag=f"wmmul_p{p}i{i}q{q}")

    def wb_casts_half(p, i, h):
        dst = wb[i % 2]
        for q in (2 * h, 2 * h + 1):
            sl = q % 2
            B.op("scalar",
                 lambda e, q=q, sl=sl, dst=dst: e.copy(
                     dst[:, q * 1024:(q + 1) * 1024], wbstage[sl][:]),
                 reads=[f"wbst{sl}"], writes=[f"wb{i % 2}"],
                 tag=f"wbcast_p{p}i{i}q{q}")

    def w_prep_serial(p, i):
        for h in range(2):
            w_loads_half(p, i, h)
            wm_mults_half(p, i, h)
            wb_casts_half(p, i, h)

    def plan_ln(p):
        for kc in range(NK):
            sl = kc % 2
            dma(f"sx{sl}", xbuf[sl][:],
                d_xt[kc * 128:(kc + 1) * 128, p * TP:(p + 1) * TP],
                writes=[f"xb{sl}"], tag=f"xt_p{p}kc{kc}")
            if p == 0:
                mb = 4 + 2 * (kc % 2)
                band_mask_half(kc, 0)
                band_mask_half(kc, 1)
                band_pooledmm(kc, mb)
                band_pevac(kc, mb)
            B.op("scalar",
                 lambda e, sl=sl, kc=kc: e.copy(xn[:, kc * TP:(kc + 1) * TP],
                                                xbuf[sl][:]),
                 reads=[f"xb{sl}"], writes=[f"xn{kc}"], tag=f"xc_p{p}kc{kc}")
            B.op("scalar",
                 lambda e, sl=sl: e.square(sq[:, sl * 1024:(sl + 1) * 1024],
                                           xbuf[sl][:]),
                 reads=[f"xb{sl}"], writes=[f"sq{sl}"], tag=f"xs_p{p}kc{kc}")
            for tb2 in range(2):
                wr = [f"pb{tb2}"] if kc in (0, NK - 1) else []
                B.op("tensor",
                     lambda e, kc=kc, tb2=tb2: e.matmul(
                         pb[tb2][0:1, :], onesb[:],
                         xn[:, kc * TP + tb2 * TB: kc * TP + (tb2 + 1) * TB],
                         start=(kc == 0), stop=(kc == NK - 1)),
                     reads=[f"xn{kc}"], writes=wr,
                     inc=("pe", 1),
                     tag=f"mmsum_p{p}k{kc}t{tb2}")
                wr = [f"pb{2 + tb2}"] if kc in (0, NK - 1) else []
                B.op("tensor",
                     lambda e, kc=kc, tb2=tb2, sl=sl: e.matmul(
                         pb[2 + tb2][0:1, :], onesb[:],
                         sq[:, sl * 1024 + tb2 * TB: sl * 1024 + (tb2 + 1) * TB],
                         start=(kc == 0), stop=(kc == NK - 1)),
                     reads=[f"sq{sl}"], writes=wr,
                     inc=("pe", 1),
                     tag=f"mmsq_p{p}k{kc}t{tb2}")
        if p == 0:
            bounce_out_all()
            for h in range(2):
                bounce_in_half(h)
                for bb in range(h * 16, (h + 1) * 16):
                    band_e4l(bb, 4 + 2 * (bb % 2))
                    band_texp_evac(bb, 4 + 2 * (bb % 2))
        for tb2 in range(2):
            B.op("vector",
                 lambda e, tb2=tb2: e.tensor_copy(srow[:], pb[tb2][0:1, :]),
                 reads=[f"pb{tb2}"], writes=["srow"], tag=f"evs_p{p}t{tb2}")
            B.op("vector",
                 lambda e, tb2=tb2: e.tensor_copy(qrow[:], pb[2 + tb2][0:1, :]),
                 reads=[f"pb{2 + tb2}"], writes=["qrow"], tag=f"evq_p{p}t{tb2}")
            mbk, qbk = 4 + 2 * tb2, 5 + 2 * tb2
            B.op("tensor",
                 lambda e, mbk=mbk: e.matmul(pb[mbk][:], ones1[:], srow[:],
                                             start=True, stop=True),
                 reads=["srow"], writes=[f"pb{mbk}"], inc=("pe", 1),
                 tag=f"mmbm_p{p}t{tb2}")
            B.op("tensor",
                 lambda e, qbk=qbk: e.matmul(pb[qbk][:], ones1[:], qrow[:],
                                             start=True, stop=True),
                 reads=["qrow"], writes=[f"pb{qbk}"], inc=("pe", 1),
                 tag=f"mmbq_p{p}t{tb2}")
            B.op("vector",
                 lambda e, mbk=mbk: e.tensor_scalar_mul(lnA, pb[mbk][:],
                                                        1.0 / D),
                 reads=[f"pb{mbk}"], writes=["g0"], tag=f"mb_p{p}t{tb2}")
            B.op("scalar", lambda e: e.square(lnB, lnA),
                 reads=["g0"], writes=["g1"], tag=f"m2b_p{p}t{tb2}")
            B.op("vector",
                 lambda e, qbk=qbk: e.tensor_scalar(
                     lnC, pb[qbk][:], 1.0 / D, EPS_LN, ALU.mult, ALU.add),
                 reads=[f"pb{qbk}"], writes=["sg0"], tag=f"qe_p{p}t{tb2}")
            B.op("vector",
                 lambda e: e.tensor_sub(lnD, lnC, lnB),
                 reads=["sg0", "g1"], writes=["sg1"], tag=f"var_p{p}t{tb2}")
            B.op("scalar",
                 lambda e: e.activation(lnB, lnD, AF.Sqrt),
                 reads=["sg1"], writes=["g1"], tag=f"sd_p{p}t{tb2}")
            B.op("vector", lambda e: e.reciprocal(lnC, lnB),
                 reads=["g1"], writes=["sg0"], tag=f"rr_p{p}t{tb2}")
            B.op("vector", lambda e: e.tensor_mul(lnB, lnA, lnC),
                 reads=["g0", "sg0"], writes=["g1"], tag=f"mrr_p{p}t{tb2}")
            B.op("scalar", lambda e, tb2=tb2: e.copy(rb_v[tb2], lnC),
                 reads=["sg0"], writes=["sq0"], tag=f"rbf_p{p}t{tb2}")
            B.op("scalar", lambda e, tb2=tb2: e.copy(mrb_v[tb2], lnB),
                 reads=["g1"], writes=["sq1"], tag=f"mrbf_p{p}t{tb2}")
        for kc in range(NK):
            xsl = xn[:, kc * TP:(kc + 1) * TP]
            B.op("vector",
                 lambda e, xsl=xsl: e.tensor_mul(txn[:], xsl, rb_all),
                 reads=[f"xn{kc}", "sq0"], writes=["txn"],
                 tag=f"n1_p{p}k{kc}")
            B.op("vector",
                 lambda e, xsl=xsl: e.tensor_sub(xsl, txn[:], mrb_all),
                 reads=["txn", "sq1"], writes=[f"xn{kc}"],
                 tag=f"n2_p{p}k{kc}")

    def plan_p2(p):
        for tb2 in range(2):
            zfns, zrd, lfns, lrd = [], [], [], []
            for kc in range(NK):
                def zf(e, kc=kc, tb2=tb2):
                    return e.matmul(
                        pb[tb2][:], lA[:, kc * 128:(kc + 1) * 128],
                        xn[:, kc * TP + tb2 * TB: kc * TP + (tb2 + 1) * TB],
                        start=(kc == 0), stop=(kc == NK - 1))
                zfns.append(zf)
                zrd.append(["lA", f"xn{kc}"])

                def lf(e, kc=kc, tb2=tb2):
                    return e.matmul(
                        pb[2 + tb2][0:16, :], lR[:, kc * 16:(kc + 1) * 16],
                        xn[:, kc * TP + tb2 * TB: kc * TP + (tb2 + 1) * TB],
                        start=(kc == 0), stop=(kc == NK - 1))
                lfns.append(lf)
                lrd.append(["lR", f"xn{kc}"])
            B.acc_group("tensor", zfns, zrd, [f"pb{tb2}"], inc=("pe", 1),
                        tag=f"mmz_p{p}t{tb2}")
            B.acc_group("tensor", lfns, lrd, [f"pb{2 + tb2}"], inc=("pe", 1),
                        tag=f"mmlg_p{p}t{tb2}")
            B.op("vector",
                 lambda e, tb2=tb2: e.tensor_copy(
                     z_sb[:, tb2 * TB:(tb2 + 1) * TB], pb[tb2][:]),
                 reads=[f"pb{tb2}"], writes=["z_sb"], tag=f"evz_p{p}t{tb2}")


    def plan_p2_alpha(p, tb2):
        bk = [(4, 5, 6, 7), (0, 1, 2, 3)][tb2]
        srw = srow if tb2 == 0 else qrow
        srn = "srow" if tb2 == 0 else "qrow"
        B.op("vector",
             lambda e, tb2=tb2: e.tensor_copy(lg_tmp2[tb2][:],
                                              pb[2 + tb2][0:16, :]),
             reads=[f"pb{2 + tb2}"], writes=["lg"], tag=f"evlg_p{p}t{tb2}")
        B.op("tensor",
             lambda e, tb2=tb2, bk=bk: e.matmul(pb[bk[0]][0:4, :], st_sb[:],
                                                lg_tmp2[tb2][:],
                                                start=True, stop=True),
             reads=["st", "lg"], writes=[f"pb{bk[0]}"], inc=("pe", 1),
             tag=f"mmlsel_p{p}t{tb2}")
        B.op("scalar",
             lambda e, tb2=tb2, bk=bk: e.activation(esb2[tb2][:],
                                                    pb[bk[0]][0:4, :], AF.Exp,
                                                    bias=ce_sb[:], scale=1.0),
             reads=[f"pb{bk[0]}", "ce"], writes=[f"esb{tb2}"],
             tag=f"exp_p{p}t{tb2}")
        B.op("tensor",
             lambda e, tb2=tb2, bk=bk: e.matmul(pb[bk[1]][0:1, :], ones4[:],
                                                esb2[tb2][:],
                                                start=True, stop=True),
             reads=[f"esb{tb2}"], writes=[f"pb{bk[1]}"], inc=("pe", 1),
             tag=f"mmss_p{p}t{tb2}")
        B.op("vector",
             lambda e, bk=bk, srw=srw: e.tensor_copy(srw[0:1, :],
                                                     pb[bk[1]][0:1, :]),
             reads=[f"pb{bk[1]}"], writes=[srn], tag=f"evss_p{p}t{tb2}")
        B.op("vector", lambda e, srw=srw: e.reciprocal(srw[0:1, :], srw[0:1, :]),
             reads=[srn], writes=[srn], tag=f"rs_p{p}t{tb2}")
        B.op("tensor",
             lambda e, bk=bk, srw=srw: e.matmul(pb[bk[2]][:], ones1[:],
                                                srw[0:1, :],
                                                start=True, stop=True),
             reads=[srn], writes=[f"pb{bk[2]}"], inc=("pe", 1),
             tag=f"mmrsb_p{p}t{tb2}")
        B.op("vector",
             lambda e, tb2=tb2, bk=bk: e.tensor_copy(rs_bf[tb2][:],
                                                     pb[bk[2]][:]),
             reads=[f"pb{bk[2]}"], writes=[f"rsbf{tb2}"],
             tag=f"evrs_p{p}t{tb2}")
        B.op("tensor",
             lambda e, tb2=tb2, bk=bk: e.matmul(pb[bk[3]][:], m2f[:],
                                                esb2[tb2][:],
                                                start=True, stop=True),
             reads=["m2f", f"esb{tb2}"], writes=[f"pb{bk[3]}"], inc=("pe", 1),
             tag=f"mmau_p{p}t{tb2}")
        B.op("vector",
             lambda e, tb2=tb2, bk=bk: e.tensor_scalar_mul(aexp2[tb2][:],
                                                           pb[bk[3]][:],
                                                           SCALING),
             reads=[f"pb{bk[3]}"], writes=[f"aexp{tb2}"],
             tag=f"au2_p{p}t{tb2}")
        txv = txn[:, tb2 * TB:(tb2 + 1) * TB]
        B.op("vector",
             lambda e, tb2=tb2, txv=txv: e.tensor_mul(
                 txv, z_sb[:, tb2 * TB:(tb2 + 1) * TB], aexp2[tb2][:]),
             reads=["z_sb", f"aexp{tb2}"], writes=["txn"],
             tag=f"zw1_p{p}t{tb2}")
        B.op("vector",
             lambda e, tb2=tb2, txv=txv: e.tensor_mul(
                 zw[:, tb2 * TB:(tb2 + 1) * TB], txv, rs_bf[tb2][:]),
             reads=["txn", f"rsbf{tb2}"], writes=[f"zw{tb2}"],
             tag=f"zw2_p{p}t{tb2}")

    def plan_iter(p, i):
        par = i % 2
        base = par * 4
        mini_bank = (1 - par) * 4 + 3
        banding = (p == 0)

        if i + 1 < NN:
            sl = (i + 1) % 2
            dma(f"sx{sl}", xbuf[sl][:],
                d_xt[(i + 1) * 128:(i + 2) * 128, p * TP:(p + 1) * TP],
                writes=[f"xb{sl}"], tag=f"xr_p{p}i{i + 1}")
            w_loads_half(p, i + 1, 0)

        if banding and 3 <= i + 3 <= 31:
            band_mask_half(i + 3, 0)

        for tb in range(2):
            sbank = base + tb
            ybank = base + 2 + tb
            fns, rds = [], []
            for kc in range(NK):
                def f(e, kc=kc, tb=tb, sbank=sbank, i=i):
                    return e.matmul(
                        pb[sbank][:], wm[i % 2][:, kc * 128:(kc + 1) * 128],
                        xn[:, kc * TP + tb * TB: kc * TP + (tb + 1) * TB],
                        start=(kc == 0), stop=(kc == NK - 1))
                fns.append(f)
                rds.append([f"wm{i % 2}", f"xn{kc}"])
            B.acc_group("tensor", fns, rds, [f"pb{sbank}"], inc=("pe", 1),
                        tag=f"mms_p{p}i{i}t{tb}")

            if tb == 0:
                if i + 1 < NN:
                    wm_mults_half(p, i + 1, 0)
                    wb_casts_half(p, i + 1, 0)
                    w_loads_half(p, i + 1, 1)

            fns, rds = [], []
            for kc in range(NK):
                def f(e, kc=kc, tb=tb, ybank=ybank, i=i):
                    return e.matmul(
                        pb[ybank][:], wb[i % 2][:, kc * 128:(kc + 1) * 128],
                        xn[:, kc * TP + tb * TB: kc * TP + (tb + 1) * TB],
                        start=(kc == 0), stop=False)
                fns.append(f)
                rds.append([f"wb{i % 2}", f"xn{kc}"])

            def fB(e, tb=tb, ybank=ybank, i=i):
                return e.matmul(pb[ybank][:], lB[:, i * 128:(i + 1) * 128],
                                zw[:, tb * TB:(tb + 1) * TB],
                                start=False, stop=True)
            fns.append(fB)
            rds.append(["lB", f"zw{tb}"])
            B.acc_group("tensor", fns, rds, [f"pb{ybank}"], inc=("pe", 1),
                        tag=f"mmy_p{p}i{i}t{tb}")

            if tb == 1 and i + 1 < NN:
                wm_mults_half(p, i + 1, 1)
                wb_casts_half(p, i + 1, 1)

            B.op("scalar",
                 lambda e, tb=tb, ybank=ybank, i=i: e.activation(
                     g_sb[:, tb * TB:(tb + 1) * TB], pb[ybank][:], AF.Silu,
                     bias=b_sb[:, i:i + 1], scale=1.0),
                 reads=[f"pb{ybank}", "b_sb"], writes=[f"g{tb}"],
                 tag=f"silu_p{p}i{i}t{tb}")
            B.op("vector",
                 lambda e, tb=tb, sbank=sbank: e.tensor_mul(
                     sg_sb[:, tb * TB:(tb + 1) * TB], pb[sbank][:],
                     g_sb[:, tb * TB:(tb + 1) * TB]),
                 reads=[f"pb{sbank}", f"g{tb}"], writes=[f"sg{tb}"],
                 tag=f"sg_p{p}i{i}t{tb}")
            B.op("gpsimd",
                 lambda e, tb=tb, i=i: e.tensor_tensor(
                     out_sb[i % 2][:, tb * TB:(tb + 1) * TB],
                     sg_sb[:, tb * TB:(tb + 1) * TB],
                     xbuf[i % 2][:, tb * TB:(tb + 1) * TB], ALU.add),
                 reads=[f"sg{tb}", f"xb{i % 2}"], writes=[f"o{i % 2}"],
                 tag=f"res_p{p}i{i}t{tb}")

        gdma(f"so{i % 2}", d_out[i * 128:(i + 1) * 128, p * TP:(p + 1) * TP],
             out_sb[i % 2][:], reads=[f"o{i % 2}"], tag=f"st_p{p}i{i}")

    plan_startup()
    for p in range(NPASS):
        plan_ln(p)
        plan_p2(p)
        w_prep_serial(p, 0)
        plan_p2_alpha(p, 0)
        plan_p2_alpha(p, 1)
        dma("sx0", xbuf[0][:], d_xt[0:128, p * TP:(p + 1) * TP],
            writes=["xb0"], tag=f"xr_p{p}i0")
        for i in range(NN):
            plan_iter(p, i)

    B.finalize()
    fin = []
    for s in ("so0", "so1", "sbb0", "sbb1", "sm0", "sm1"):
        if s in B.counts:
            fin.append(B.ev_at(s, B.counts[s]))
    B.ops["gpsimd"].append(Op("gpsimd", "WAIT_ONLY", fin, None, "final_wait"))
    B.validate()

    semh = {s: E(nc.semaphore(s)) for s in sorted(B.semnames)}
    blk = E(nc.Block())

    def emitter(name):
        def run(eng):
            hw = {}
            own = B.ENG_SEM.get(name)
            for o in B.ops[name]:
                for w in o.waits:
                    if hw.get(w.sem, -1) >= w.count:
                        continue
                    eng.wait_ge(semh[w.sem], w.count)
                    hw[w.sem] = w.count
                if o.fn == "NOP_INC":
                    if name == "tensor":
                        ins = eng.matmul(pb[0][0:1, 0:1], ones1[0:1, 0:1],
                                         ones1[0:1, 0:1], start=True, stop=True)
                    elif name == "gpsimd":
                        ins = eng.memset(sgn_t[0:1, 0:1], 0.0)
                    elif name == "scalar":
                        ins = eng.copy(sgn_t[0:1, 0:1], sgn_t[0:1, 0:1])
                    else:
                        ins = eng.tensor_copy(sgn_t[0:1, 0:1], sgn_t[0:1, 0:1])
                    ins.then_inc(semh[o.inc[0]], o.inc[1])
                    continue
                if o.fn == "WAIT_ONLY":
                    continue
                ins = o.fn(eng)
                if o.inc is not None:
                    ins.then_inc(semh[o.inc[0]], o.inc[1])
        return run

    blk.sync(emitter("sync"))
    blk.tensor(emitter("tensor"))
    blk.vector(emitter("vector"))
    blk.scalar(emitter("scalar"))
    blk.gpsimd(emitter("gpsimd"))
    ctx.close()
    return nc


def kernel(L0_attn_data, L0_attn_weight, L0_attn_mask, L0_ffn_W_base,
           L0_ffn_b_base, L0_ffn_lora_A, L0_ffn_lora_B, L0_ffn_router_W,
           L0_ffn_router_b, L0_ffn_domain_prior, L0_ffn_active):
    x = np.ascontiguousarray(np.asarray(L0_attn_data, np.float32)
                             .reshape(16384, D))
    W = np.asarray(L0_attn_weight, np.float32)
    mask = np.asarray(L0_attn_mask, np.int32)
    Wb = np.asarray(L0_ffn_W_base, np.float32)
    bbase = np.asarray(L0_ffn_b_base, np.float32)
    lora_A = np.asarray(L0_ffn_lora_A, np.float32)
    lora_B = np.asarray(L0_ffn_lora_B, np.float32)
    rW = np.asarray(L0_ffn_router_W, np.float32)
    rb = np.asarray(L0_ffn_router_b, np.float32)
    prior = np.asarray(L0_ffn_domain_prior, np.float32)
    active = np.asarray(L0_ffn_active, np.int32)

    def prep2(M):
        return np.ascontiguousarray(
            M.reshape(NN, 128, NK, 128).transpose(0, 3, 2, 1)
            .reshape(NN, 128, D))

    wp = prep2(W)
    wbp = prep2(Wb)
    mkp = np.ascontiguousarray(
        mask.reshape(D, 4, 32, 32).transpose(0, 1, 3, 2).reshape(D, D))
    ap_ = np.ascontiguousarray(
        lora_A.transpose(2, 0, 1).reshape(D, 128).reshape(NK, 128, 128)
        .transpose(1, 0, 2).reshape(128, D))
    rp = np.ascontiguousarray(
        rW.T.reshape(NK, 128, 16).transpose(1, 0, 2).reshape(128, 512))
    bp = np.ascontiguousarray(lora_B.transpose(0, 2, 1).reshape(128, D))

    e4l = np.zeros((4, 128), BF)
    for j in range(4):
        e4l[j, j * 32:(j + 1) * 32] = 1
    e4p = np.ascontiguousarray(e4l.T)
    ekron = np.zeros((16, 128), np.float32)
    for Ei in range(16):
        ekron[Ei, Ei * 8:(Ei + 1) * 8] = 1

    common = dict(
        wp=wp, wbp=wbp, mkp=mkp, ap_=ap_, rp=rp, bp=bp,
        bb=np.ascontiguousarray(bbase.reshape(32, 128)),
        rbv=np.ascontiguousarray(rb.reshape(16, 1)),
        pri=np.ascontiguousarray(prior.reshape(16, 1)),
        actv=np.ascontiguousarray(active.reshape(1, 4)),
        ident=np.eye(32, dtype=np.float32),
        ones1f=np.ones((1, 128), np.float32),
        onesbf=np.ones((128, 1), BF),
        e4l=e4l, e4p=e4p, ekron=ekron,
        iota16=np.arange(16, dtype=np.float32).reshape(1, 16),
        ones4f=np.ones((4, 1), np.float32),
    )
    in_maps = []
    for c in range(NCORE):
        m = dict(common)
        m["xt"] = np.ascontiguousarray(x[c * TPC:(c + 1) * TPC].T)
        in_maps.append(m)

    nc = build_program()
    res = run_bass_kernel_spmd(nc, in_maps, core_ids=list(range(NCORE)))
    out = np.concatenate([np.asarray(r["out"]) for r in res.results], axis=1)
    return np.ascontiguousarray(out.T).reshape(2, 8192, D).astype(np.float32)


if __name__ == "__main__":
    build_program()
    print("build ok")


# version 22
# speedup vs baseline: 1.1854x; 1.1854x over previous
import numpy as np
import ml_dtypes
import concourse.bass as bass
import concourse.mybir as mybir
from concourse.bass_utils import run_bass_kernel_spmd

F32 = mybir.dt.float32
BF16 = mybir.dt.bfloat16
I32 = mybir.dt.int32
AF = mybir.ActivationFunctionType
ALU = mybir.AluOpType
BF = ml_dtypes.bfloat16

NCORE = 8
D = 4096
NK = 32
NN = 32
TPC = 2048
TP = 1024
NPASS = 2
TB = 512
EPS_LN = 1e-5
EPS_PRIOR = 1e-12
SCALING = 2.0


class Ev:
    __slots__ = ("sem", "count")

    def __init__(self, sem, count=None):
        self.sem = sem
        self.count = count


class Res:
    __slots__ = ("last_w", "readers", "name")

    def __init__(self, name):
        self.name = name
        self.last_w = None
        self.readers = []


class Op:
    __slots__ = ("eng", "fn", "waits", "inc", "tag")

    def __init__(self, eng, fn, waits, inc, tag):
        self.eng, self.fn, self.waits, self.inc, self.tag = \
            eng, fn, waits, inc, tag


class Builder:
    ENGS = ("sync", "tensor", "vector", "scalar", "gpsimd")
    ENG_SEM = {"tensor": "pe", "vector": "dve", "scalar": "act",
               "gpsimd": "pool"}

    def __init__(self):
        self.ops = {e: [] for e in self.ENGS}
        self.counts = {}
        self.pending = {e: [] for e in self.ENGS}
        self.res = {}
        self.semnames = set()

    def r(self, name):
        if name not in self.res:
            self.res[name] = Res(name)
        return self.res[name]

    def sem(self, name):
        self.semnames.add(name)
        self.counts.setdefault(name, 0)
        return name

    def ev_at(self, semname, count):
        return Ev(self.sem(semname), count)

    def _mkop(self, eng, fn, wl, inc, tag):
        if inc is not None:
            semname, amt = inc
            self.sem(semname)
            self.counts[semname] += amt
            ev = Ev(semname, self.counts[semname])
            if semname == self.ENG_SEM.get(eng):
                for p in self.pending[eng]:
                    p.count = self.counts[semname]
                self.pending[eng] = []
        else:
            ev = Ev(self.sem(self.ENG_SEM[eng]))
            self.pending[eng].append(ev)
        self.ops[eng].append(Op(eng, fn, wl, inc, tag))
        return ev

    def op(self, eng, fn, reads=(), writes=(), waits=(), inc=None, tag=""):
        if inc is None and eng in ("vector", "scalar", "gpsimd"):
            inc = (self.ENG_SEM[eng], 1)
        reads = [self.r(x) if isinstance(x, str) else x for x in reads]
        writes = [self.r(x) if isinstance(x, str) else x for x in writes]
        wl = list(waits)
        for rs in reads:
            if rs.last_w is not None:
                wl.append(rs.last_w)
        for rs in writes:
            if rs.last_w is not None:
                wl.append(rs.last_w)
            wl.extend(rs.readers)
        ev = self._mkop(eng, fn, wl, inc, tag)
        for rs in reads:
            rs.readers.append(ev)
        for rs in writes:
            rs.last_w = ev
            rs.readers = []
        return ev

    def acc_group(self, eng, fns, reads_each, writes, waits=(), inc=None,
                  tag=""):
        writes = [self.r(x) if isinstance(x, str) else x for x in writes]
        wl0 = list(waits)
        for rs in writes:
            if rs.last_w is not None:
                wl0.append(rs.last_w)
            wl0.extend(rs.readers)
        n = len(fns)
        for j, fn in enumerate(fns):
            reads = [self.r(x) if isinstance(x, str) else x
                     for x in reads_each[j]]
            wl = list(wl0) if j == 0 else []
            for rs in reads:
                if rs.last_w is not None:
                    wl.append(rs.last_w)
            ev = self._mkop(eng, fn, wl, inc if j == n - 1 else None,
                            f"{tag}[{j}]")
            for rs in reads:
                rs.readers.append(ev)
        for rs in writes:
            rs.last_w = ev
            rs.readers = []
        return ev

    def finalize(self):
        for eng in ("tensor", "vector", "scalar", "gpsimd"):
            if self.pending[eng]:
                semn = self.sem(self.ENG_SEM[eng])
                self.counts[semn] += 1
                for p in self.pending[eng]:
                    p.count = self.counts[semn]
                self.pending[eng] = []
                self.ops[eng].append(Op(eng, "NOP_INC", [], (semn, 1),
                                        "final_resolve"))

    def validate(self):
        for e in self.ENGS:
            for o in self.ops[e]:
                for w in o.waits:
                    assert w.count is not None, \
                        f"unresolved wait in {e}:{o.tag} (sem {w.sem})"
        ptr = {e: 0 for e in self.ENGS}
        cnt = {s: 0 for s in self.semnames}
        total = sum(len(v) for v in self.ops.values())
        done = 0
        progress = True
        while progress:
            progress = False
            for e in self.ENGS:
                own = self.ENG_SEM.get(e)
                while ptr[e] < len(self.ops[e]):
                    o = self.ops[e][ptr[e]]
                    if any(cnt[w.sem] < w.count for w in o.waits):
                        break
                    if o.inc is not None:
                        cnt[o.inc[0]] += o.inc[1]
                    ptr[e] += 1
                    done += 1
                    progress = True
        if done != total:
            msg = []
            for e in self.ENGS:
                if ptr[e] < len(self.ops[e]):
                    o = self.ops[e][ptr[e]]
                    bad = [(w.sem, w.count, cnt[w.sem]) for w in o.waits
                           if cnt[w.sem] < w.count]
                    msg.append(f"{e}@{ptr[e]}/{len(self.ops[e])} "
                               f"tag={o.tag} missing={bad}")
            raise RuntimeError("schedule deadlock:\n" + "\n".join(msg))


def build_program():
    nc = bass.Bass("TRN2", target_bir_lowering=False)
    B = Builder()

    d_xt = nc.dram_tensor("xt", [D, TPC], F32, kind="ExternalInput").ap()
    d_wp = nc.dram_tensor("wp", [NN, 128, D], F32, kind="ExternalInput").ap()
    d_wbp = nc.dram_tensor("wbp", [NN, 128, D], F32, kind="ExternalInput").ap()
    d_mk = nc.dram_tensor("mkp", [D, D], I32, kind="ExternalInput").ap()
    d_ap = nc.dram_tensor("ap_", [128, D], F32, kind="ExternalInput").ap()
    d_rp = nc.dram_tensor("rp", [128, 512], F32, kind="ExternalInput").ap()
    d_bp = nc.dram_tensor("bp", [128, D], F32, kind="ExternalInput").ap()
    d_bb = nc.dram_tensor("bb", [32, 128], F32, kind="ExternalInput").ap()
    d_rbv = nc.dram_tensor("rbv", [16, 1], F32, kind="ExternalInput").ap()
    d_pri = nc.dram_tensor("pri", [16, 1], F32, kind="ExternalInput").ap()
    d_act = nc.dram_tensor("actv", [1, 4], I32, kind="ExternalInput").ap()
    d_ident = nc.dram_tensor("ident", [32, 32], F32, kind="ExternalInput").ap()
    d_ones1 = nc.dram_tensor("ones1f", [1, 128], F32, kind="ExternalInput").ap()
    d_onesb = nc.dram_tensor("onesbf", [128, 1], BF16, kind="ExternalInput").ap()
    d_e4l = nc.dram_tensor("e4l", [4, 128], BF16, kind="ExternalInput").ap()
    d_e4p = nc.dram_tensor("e4p", [128, 4], BF16, kind="ExternalInput").ap()
    d_ekron = nc.dram_tensor("ekron", [16, 128], F32, kind="ExternalInput").ap()
    d_iota = nc.dram_tensor("iota16", [1, 16], F32, kind="ExternalInput").ap()
    d_ones4 = nc.dram_tensor("ones4f", [4, 1], F32, kind="ExternalInput").ap()
    d_out = nc.dram_tensor("out", [D, TPC], F32, kind="ExternalOutput").ap()
    d_scr2 = nc.dram_tensor("scr2", [128, 128], BF16, kind="Internal").ap()
    from contextlib import ExitStack
    ctx = ExitStack()
    E = ctx.enter_context

    xn = E(nc.sbuf_tensor("xn", [128, NK * TP], BF16))
    wstage = [E(nc.sbuf_tensor(f"wstage{j}", [128, 1024], F32)) for j in range(2)]
    wbstage = [E(nc.sbuf_tensor(f"wbstage{j}", [128, 1024], F32)) for j in range(2)]
    mstage = [E(nc.sbuf_tensor(f"mstage{j}", [128, 1024], I32)) for j in range(3)]
    wm = [E(nc.sbuf_tensor(f"wm{j}", [128, D], BF16)) for j in range(2)]
    wb = [E(nc.sbuf_tensor(f"wb{j}", [128, D], BF16)) for j in range(2)]
    mcB = E(nc.sbuf_tensor("mcB", [128, 512], BF16))
    mcC = E(nc.sbuf_tensor("mcC", [128, 256], BF16))
    pooled = E(nc.sbuf_tensor("pooled", [128, 128], BF16))
    bm01a = E(nc.sbuf_tensor("bm01a", [128, 128], BF16))
    sgn_t = E(nc.sbuf_tensor("sgn_t", [128, 4], F32))
    sgn_t2 = E(nc.sbuf_tensor("sgn_t2", [128, 128], BF16))
    bmra = E(nc.sbuf_tensor("bmra", [4, 2048], BF16))
    bmtexp = E(nc.sbuf_tensor("bmtexp", [128, D], BF16))
    xbuf = [E(nc.sbuf_tensor(f"xbuf{j}", [128, TP], F32)) for j in range(2)]
    out_sb = [E(nc.sbuf_tensor(f"out_sb{j}", [128, TP], F32)) for j in range(2)]
    lA = E(nc.sbuf_tensor("lA", [128, D], BF16))
    lR = E(nc.sbuf_tensor("lR", [128, 512], BF16))
    lB = E(nc.sbuf_tensor("lB", [128, D], BF16))
    z_sb = E(nc.sbuf_tensor("z_sb", [128, TP], BF16))
    zw = E(nc.sbuf_tensor("zw", [128, TP], BF16))
    txn = E(nc.sbuf_tensor("txn", [128, TP], BF16))
    sq = E(nc.sbuf_tensor("sq", [128, 2048], BF16))
    aexp2 = [E(nc.sbuf_tensor(f"aexp2_{j}", [128, TB], BF16)) for j in range(2)]
    rs_bf = [E(nc.sbuf_tensor(f"rs_bf{j}", [128, TB], BF16)) for j in range(2)]
    _lg = E(nc.sbuf_tensor("lg_tmp", [16, TB], F32))
    lg_tmp2 = [_lg, _lg]
    esb2 = [E(nc.sbuf_tensor(f"esb{j}", [4, TB], F32)) for j in range(2)]
    srow = E(nc.sbuf_tensor("srow", [1, TB], F32))
    qrow = E(nc.sbuf_tensor("qrow", [1, TB], F32))
    g_sb = E(nc.sbuf_tensor("g_sb", [128, TP], F32))
    sg_sb = E(nc.sbuf_tensor("sg_sb", [128, TP], F32))
    b_sb = E(nc.sbuf_tensor("b_sb", [128, 32], F32))
    bb_sb = E(nc.sbuf_tensor("bb_sb", [32, 128], F32))
    ident = E(nc.sbuf_tensor("ident_sb", [32, 32], F32))
    ones1 = E(nc.sbuf_tensor("ones1_sb", [1, 128], F32))
    onesb = E(nc.sbuf_tensor("onesb_sb", [128, 1], BF16))
    e4l = E(nc.sbuf_tensor("e4l_sb", [4, 128], BF16))
    e4p = E(nc.sbuf_tensor("e4p_sb", [128, 4], BF16))
    ekron = E(nc.sbuf_tensor("ekron_sb", [16, 128], F32))
    iota = E(nc.sbuf_tensor("iota_sb", [1, 16], F32))
    ones4 = E(nc.sbuf_tensor("ones4_sb", [4, 1], F32))
    actv = E(nc.sbuf_tensor("actv_sb", [1, 4], I32))
    rbv = E(nc.sbuf_tensor("rbv_sb", [16, 1], F32))
    pri = E(nc.sbuf_tensor("pri_sb", [16, 1], F32))
    a_f = E(nc.sbuf_tensor("a_f", [1, 4], F32))
    a_cl = E(nc.sbuf_tensor("a_cl", [1, 4], F32))
    a_col = E(nc.sbuf_tensor("a_col", [4, 1], F32))
    dt_sb = E(nc.sbuf_tensor("dt_sb", [4, 16], F32))
    ab_sb = E(nc.sbuf_tensor("ab_sb", [4, 16], F32))
    s_sb = E(nc.sbuf_tensor("s_sb", [4, 16], F32))
    st_sb = E(nc.sbuf_tensor("st_sb", [16, 4], F32))
    rbse = E(nc.sbuf_tensor("rbse", [4, 1], F32))
    rbse2 = E(nc.sbuf_tensor("rbse2", [4, 1], F32))
    lp_sb = E(nc.sbuf_tensor("lp_sb", [4, 1], F32))
    ce_sb = E(nc.sbuf_tensor("ce_sb", [4, 1], F32))
    m2f = E(nc.sbuf_tensor("m2f", [4, 128], F32))
    pb = [E(nc.psum_tensor(f"pb{j}", [128, 512], F32)) for j in range(8)]

    lnA = g_sb[:, 0:512]
    lnB = g_sb[:, 512:1024]
    lnC = sg_sb[:, 0:512]
    lnD = sg_sb[:, 512:1024]
    rb_v = [sq[:, 0:512], sq[:, 512:1024]]
    mrb_v = [sq[:, 1024:1536], sq[:, 1536:2048]]
    rb_all = sq[:, 0:1024]
    mrb_all = sq[:, 1024:2048]

    def dma(semn, dst, src, reads=(), writes=(), waits=(), tag=""):
        return B.op("sync", (lambda e, d=dst, s=src: e.dma_start(out=d, in_=s)),
                    reads=reads, writes=writes, waits=waits,
                    inc=(semn, 16), tag=tag)

    def adma(semn, dst, src, reads=(), writes=(), waits=(), tag=""):
        return B.op("scalar", (lambda e, d=dst, s=src: e.dma_start(out=d, in_=s)),
                    reads=reads, writes=writes, waits=waits,
                    inc=(semn, 16), tag=tag)

    def gdma(semn, dst, src, reads=(), writes=(), waits=(), tag="", slow=False):
        return B.op("gpsimd",
                    (lambda e, d=dst, s=src, sl=slow: e.dma_start(
                        out=d, in_=s, allow_slow_non_contiguous=sl)),
                    reads=reads, writes=writes, waits=waits,
                    inc=(semn, 16), tag=tag)

    def plan_startup():
        for dst, src, nm in [
            (ident[:], d_ident, "ident"), (ones1[:], d_ones1, "ones1"),
            (onesb[:], d_onesb, "onesb"), (e4l[:], d_e4l, "e4l"),
            (e4p[:], d_e4p, "e4p"), (ekron[:], d_ekron, "ekron"),
            (iota[:], d_iota, "iota"), (ones4[:], d_ones4, "ones4"),
            (actv[:], d_act, "actv"), (rbv[:], d_rbv, "rbv"),
            (pri[:], d_pri, "pri"), (bb_sb[:], d_bb, "bb"),
        ]:
            dma("init", dst, src, tag=f"ld_{nm}")
        for q in range(4):
            sl = q % 2
            dma(f"sw{sl}", wstage[sl][:], d_ap[:, q * 1024:(q + 1) * 1024],
                writes=[f"wst{sl}"], tag=f"ld_lA{q}")
            B.op("scalar",
                 lambda e, q=q, sl=sl: e.copy(lA[:, q * 1024:(q + 1) * 1024],
                                              wstage[sl][:]),
                 reads=[f"wst{sl}"], writes=["lA"], tag=f"c_lA{q}")
        for q in range(4):
            sl = q % 2
            dma(f"swb{sl}", wbstage[sl][:], d_bp[:, q * 1024:(q + 1) * 1024],
                writes=[f"wbst{sl}"], tag=f"ld_lB{q}")
            B.op("scalar",
                 lambda e, q=q, sl=sl: e.copy(lB[:, q * 1024:(q + 1) * 1024],
                                              wbstage[sl][:]),
                 reads=[f"wbst{sl}"], writes=["lB"], tag=f"c_lB{q}")
        dma("sw0", wstage[0][:, 0:512], d_rp, writes=["wst0"], tag="ld_lR")
        B.op("scalar", lambda e: e.copy(lR[:], wstage[0][:, 0:512]),
             reads=["wst0"], writes=["lR"], tag="c_lR")

        winit = [B.ev_at("init", B.counts.get("init", 0))]

        B.op("scalar", lambda e: e.copy(a_f[:], actv[:]),
             waits=winit, writes=["a_f"], tag="cast_active")
        B.op("vector",
             lambda e: e.tensor_scalar(a_cl[:], a_f[:], 0.0, 15.0,
                                       ALU.max, ALU.min),
             reads=["a_f"], writes=["a_cl"], tag="clamp_active")
        B.op("tensor",
             lambda e: e.transpose(pb[4][0:4, 0:1], a_cl[:], ident[0:1, 0:1]),
             waits=winit, reads=["a_cl"], writes=["pb4"], inc=("pe", 1),
             tag="tr_acol")
        B.op("vector", lambda e: e.tensor_copy(a_col[:], pb[4][0:4, 0:1]),
             reads=["pb4"], writes=["a_col"], tag="ev_acol")
        B.op("tensor",
             lambda e: e.matmul(pb[5][0:4, 0:16], ones1[0:1, 0:4], iota[:],
                                start=True, stop=True),
             waits=winit, writes=["pb5"], inc=("pe", 1), tag="mm_iotab")
        B.op("vector",
             lambda e: e.tensor_single_scalar(dt_sb[:], pb[5][0:4, 0:16],
                                              a_col[:], ALU.subtract),
             reads=["pb5", "a_col"], writes=["dt"], tag="d_sub")
        B.op("scalar", lambda e: e.activation(ab_sb[:], dt_sb[:], AF.Abs),
             reads=["dt"], writes=["absd"], tag="absd")
        B.op("vector",
             lambda e: e.tensor_scalar(dt_sb[:], ab_sb[:], -1.0, 1.0,
                                       ALU.mult, ALU.add),
             reads=["absd"], writes=["dt"], tag="S_lin")
        B.op("vector",
             lambda e: e.tensor_scalar_max(s_sb[:], dt_sb[:], 0.0),
             reads=["dt"], writes=["S"], tag="S_onehot")
        B.op("tensor",
             lambda e: e.transpose(pb[6][0:16, 0:4], s_sb[:], ident[0:4, 0:4]),
             reads=["S"], writes=["pb6"], inc=("pe", 1), tag="tr_S")
        B.op("vector", lambda e: e.tensor_copy(st_sb[:], pb[6][0:16, 0:4]),
             reads=["pb6"], writes=["st"], tag="ev_st")
        B.op("tensor",
             lambda e: e.matmul(pb[7][0:4, 0:1], st_sb[:], rbv[:],
                                start=True, stop=True),
             waits=winit, reads=["st"], writes=["pb7"], inc=("pe", 1),
             tag="mm_rbsel")
        B.op("tensor",
             lambda e: e.matmul(pb[7][0:4, 1:2], st_sb[:], pri[:],
                                start=True, stop=True),
             reads=["st"], writes=["pb7"], inc=("pe", 1), tag="mm_prisel")
        B.op("vector", lambda e: e.tensor_copy(rbse[:], pb[7][0:4, 0:1]),
             reads=["pb7"], writes=["rbse"], tag="ev_rbse")
        B.op("vector",
             lambda e: e.tensor_scalar_add(rbse2[:], pb[7][0:4, 1:2],
                                           EPS_PRIOR),
             reads=["pb7"], writes=["prieps"], tag="pri_eps")
        B.op("scalar",
             lambda e: e.activation(lp_sb[:], rbse2[:], AF.Ln),
             reads=["prieps"], writes=["lp"], tag="ln_prior")
        B.op("vector", lambda e: e.tensor_add(ce_sb[:], rbse[:], lp_sb[:]),
             reads=["rbse", "lp"], writes=["ce"], tag="ce")
        B.op("tensor",
             lambda e: e.matmul(pb[4][0:4, 0:128], st_sb[:], ekron[:],
                                start=True, stop=True),
             reads=["st"], writes=["pb4"], inc=("pe", 1), tag="mm_M2")
        B.op("vector", lambda e: e.tensor_copy(m2f[:], pb[4][0:4, 0:128]),
             reads=["pb4"], writes=["m2f"], tag="ev_M2")
        B.op("tensor",
             lambda e: e.transpose(pb[5][:, 0:32], bb_sb[:], ident[0:32, 0:32]),
             waits=winit, writes=["pb5"], inc=("pe", 1), tag="tr_bb")
        B.op("vector", lambda e: e.tensor_copy(b_sb[:], pb[5][:, 0:32]),
             reads=["pb5"], writes=["b_sb"], tag="ev_bb")

    def band_mask_half(b, h):
        for q in (2 * h, 2 * h + 1):
            sl = (b * 4 + q) % 3
            adma(f"sm{sl}", mstage[sl][:],
                 d_mk[b * 128:(b + 1) * 128, q * 1024:(q + 1) * 1024],
                 writes=[f"mst{sl}"], tag=f"mkq_b{b}q{q}")
            B.op("vector",
                 lambda e, sl=sl: e.tensor_add(mcB[:], mstage[sl][:, 0:512],
                                               mstage[sl][:, 512:1024]),
                 reads=[f"mst{sl}"], writes=["mcB"], tag=f"h1_b{b}q{q}")

            def fred(e, q=q):
                iv = bass.AP(mcB.ap().tensor, 0, [[512, 128], [1, 32], [32, 16]])
                ov = pooled[:, q * 32:(q + 1) * 32].rearrange(
                    "p (r one) -> p r one", one=1)
                with nc.allow_low_precision(
                        reason="mask block sums are integers <=32, exact bf16"):
                    return e.tensor_reduce(ov, iv, mybir.AxisListType.X,
                                           ALU.add)
            B.op("vector", fred, reads=["mcB"], writes=["pooled"],
                 tag=f"hr_b{b}q{q}")

    def band_load_s2(b):
        band_mask_half(b, 0)
        band_mask_half(b, 1)

    def band_pooledmm(b, bank):
        B.op("tensor",
             lambda e, bank=bank: e.matmul(pb[bank][:, 128:132], pooled[:],
                                           e4p[:], start=True, stop=True),
             reads=["pooled"], writes=[f"pb{bank}"], inc=("pe", 1),
             tag=f"poolmm_b{b}")

    def band_pevac(b, bank):
        B.op("vector",
             lambda e, bank=bank, b=b: e.tensor_copy(
                 bm01a[:, b * 4:(b + 1) * 4], pb[bank][:, 128:132]),
             reads=[f"pb{bank}"], writes=["bm01a"], tag=f"pev_b{b}")

    def bounce_out_all():
        dma("sbb0", d_scr2, bm01a[:], reads=["bm01a"], writes=["scr2"],
            tag="bout_all")

    def bounce_in_half(h):
        dst = bmra.ap().rearrange("j (kc x) -> j kc x", kc=32)
        src = d_scr2.rearrange("(kc j) x -> j kc x", j=4)[:, :, h * 64:(h + 1) * 64]
        dma("sbb0", dst, src, reads=["scr2"], writes=["bmra"],
            tag=f"bin_h{h}")

    def band_e4l(b, bank):
        bl = b % 16

        def fn(e, bank=bank, bl=bl):
            rhs = bass.AP(bmra.ap().tensor, bl * 4, [[2048, 4], [64, 32], [1, 4]])
            out = pb[bank][:, 0:128].rearrange("p (kc nb) -> p kc nb", nb=4)
            return e.matmul(out, e4l[:], rhs, start=True, stop=True)
        B.op("tensor", fn, reads=["bmra"], writes=[f"pb{bank}"],
             inc=("pe", 1), tag=f"e4l_b{b}")

    def band_texp_evac(b, bank):
        B.op("scalar",
             lambda e, bank=bank: e.activation(sgn_t2[:], pb[bank][:, 0:128],
                                               AF.Sign),
             reads=[f"pb{bank}"], writes=["sgn2"], tag=f"tsgn_b{b}")
        B.op("scalar",
             lambda e, b=b: e.activation(bmtexp[:, b * 128:(b + 1) * 128],
                                         sgn_t2[:], AF.Relu),
             reads=["sgn2"], writes=[f"texp{b}"], tag=f"texp_b{b}")

    def w_loads_half(p, i, h):
        for q in (2 * h, 2 * h + 1):
            sl = q % 2
            dma(f"sw{sl}", wstage[sl][:], d_wp[i][:, q * 1024:(q + 1) * 1024],
                writes=[f"wst{sl}"], tag=f"Wq_p{p}i{i}q{q}")
        for q in (2 * h, 2 * h + 1):
            sl = q % 2
            adma(f"swb{sl}", wbstage[sl][:],
                 d_wbp[i][:, q * 1024:(q + 1) * 1024],
                 writes=[f"wbst{sl}"], tag=f"WBq_p{p}i{i}q{q}")

    def wm_mults_half(p, i, h):
        dst = wm[i % 2]
        for q in (2 * h, 2 * h + 1):
            sl = q % 2

            def fn(e, q=q, sl=sl, dst=dst, i=i):
                o = dst[:, q * 1024:(q + 1) * 1024].rearrange(
                    "p (kc a b) -> p kc a b", a=4, b=32)
                i1 = wstage[sl][:].rearrange("p (kc a b) -> p kc a b",
                                             a=4, b=32)
                i2 = bass.AP(bmtexp.ap().tensor, i * 128 + q * 32,
                             [[D, 128], [4, 8], [1, 4], [0, 32]])
                return e.tensor_tensor(o, i1, i2, ALU.mult)
            B.op("vector", fn, reads=[f"wst{sl}", f"texp{i}"],
                 writes=[f"wm{i % 2}"], tag=f"wmmul_p{p}i{i}q{q}")

    def wb_casts_half(p, i, h):
        dst = wb[i % 2]
        for q in (2 * h, 2 * h + 1):
            sl = q % 2
            B.op("scalar",
                 lambda e, q=q, sl=sl, dst=dst: e.copy(
                     dst[:, q * 1024:(q + 1) * 1024], wbstage[sl][:]),
                 reads=[f"wbst{sl}"], writes=[f"wb{i % 2}"],
                 tag=f"wbcast_p{p}i{i}q{q}")

    def w_prep_serial(p, i):
        for h in range(2):
            w_loads_half(p, i, h)
            wm_mults_half(p, i, h)
            wb_casts_half(p, i, h)

    def plan_ln(p):
        for kc in range(NK):
            sl = kc % 2
            dma(f"sx{sl}", xbuf[sl][:],
                d_xt[kc * 128:(kc + 1) * 128, p * TP:(p + 1) * TP],
                writes=[f"xb{sl}"], tag=f"xt_p{p}kc{kc}")
            if p == 0:
                mb = 4 + 2 * (kc % 2)
                band_mask_half(kc, 0)
                band_mask_half(kc, 1)
                band_pooledmm(kc, mb)
                band_pevac(kc, mb)
            B.op("scalar",
                 lambda e, sl=sl, kc=kc: e.copy(xn[:, kc * TP:(kc + 1) * TP],
                                                xbuf[sl][:]),
                 reads=[f"xb{sl}"], writes=[f"xn{kc}"], tag=f"xc_p{p}kc{kc}")
            B.op("scalar",
                 lambda e, sl=sl: e.square(sq[:, sl * 1024:(sl + 1) * 1024],
                                           xbuf[sl][:]),
                 reads=[f"xb{sl}"], writes=[f"sq{sl}"], tag=f"xs_p{p}kc{kc}")
            for tb2 in range(2):
                wr = [f"pb{tb2}"] if kc in (0, NK - 1) else []
                B.op("tensor",
                     lambda e, kc=kc, tb2=tb2: e.matmul(
                         pb[tb2][0:1, :], onesb[:],
                         xn[:, kc * TP + tb2 * TB: kc * TP + (tb2 + 1) * TB],
                         start=(kc == 0), stop=(kc == NK - 1)),
                     reads=[f"xn{kc}"], writes=wr,
                     inc=("pe", 1),
                     tag=f"mmsum_p{p}k{kc}t{tb2}")
                wr = [f"pb{2 + tb2}"] if kc in (0, NK - 1) else []
                B.op("tensor",
                     lambda e, kc=kc, tb2=tb2, sl=sl: e.matmul(
                         pb[2 + tb2][0:1, :], onesb[:],
                         sq[:, sl * 1024 + tb2 * TB: sl * 1024 + (tb2 + 1) * TB],
                         start=(kc == 0), stop=(kc == NK - 1)),
                     reads=[f"sq{sl}"], writes=wr,
                     inc=("pe", 1),
                     tag=f"mmsq_p{p}k{kc}t{tb2}")
        if p == 0:
            bounce_out_all()
            for h in range(2):
                bounce_in_half(h)
                for bb in range(h * 16, (h + 1) * 16):
                    band_e4l(bb, 4 + 2 * (bb % 2))
                    band_texp_evac(bb, 4 + 2 * (bb % 2))
        for tb2 in range(2):
            B.op("vector",
                 lambda e, tb2=tb2: e.tensor_copy(srow[:], pb[tb2][0:1, :]),
                 reads=[f"pb{tb2}"], writes=["srow"], tag=f"evs_p{p}t{tb2}")
            B.op("vector",
                 lambda e, tb2=tb2: e.tensor_copy(qrow[:], pb[2 + tb2][0:1, :]),
                 reads=[f"pb{2 + tb2}"], writes=["qrow"], tag=f"evq_p{p}t{tb2}")
            mbk, qbk = 4 + 2 * tb2, 5 + 2 * tb2
            B.op("tensor",
                 lambda e, mbk=mbk: e.matmul(pb[mbk][:], ones1[:], srow[:],
                                             start=True, stop=True),
                 reads=["srow"], writes=[f"pb{mbk}"], inc=("pe", 1),
                 tag=f"mmbm_p{p}t{tb2}")
            B.op("tensor",
                 lambda e, qbk=qbk: e.matmul(pb[qbk][:], ones1[:], qrow[:],
                                             start=True, stop=True),
                 reads=["qrow"], writes=[f"pb{qbk}"], inc=("pe", 1),
                 tag=f"mmbq_p{p}t{tb2}")
            B.op("vector",
                 lambda e, mbk=mbk: e.tensor_scalar_mul(lnA, pb[mbk][:],
                                                        1.0 / D),
                 reads=[f"pb{mbk}"], writes=["g0"], tag=f"mb_p{p}t{tb2}")
            B.op("scalar", lambda e: e.square(lnB, lnA),
                 reads=["g0"], writes=["g1"], tag=f"m2b_p{p}t{tb2}")
            B.op("vector",
                 lambda e, qbk=qbk: e.tensor_scalar(
                     lnC, pb[qbk][:], 1.0 / D, EPS_LN, ALU.mult, ALU.add),
                 reads=[f"pb{qbk}"], writes=["sg0"], tag=f"qe_p{p}t{tb2}")
            B.op("vector",
                 lambda e: e.tensor_sub(lnD, lnC, lnB),
                 reads=["sg0", "g1"], writes=["sg1"], tag=f"var_p{p}t{tb2}")
            B.op("scalar",
                 lambda e: e.activation(lnB, lnD, AF.Sqrt),
                 reads=["sg1"], writes=["g1"], tag=f"sd_p{p}t{tb2}")
            B.op("vector", lambda e: e.reciprocal(lnC, lnB),
                 reads=["g1"], writes=["sg0"], tag=f"rr_p{p}t{tb2}")
            B.op("vector", lambda e: e.tensor_mul(lnB, lnA, lnC),
                 reads=["g0", "sg0"], writes=["g1"], tag=f"mrr_p{p}t{tb2}")
            B.op("scalar", lambda e, tb2=tb2: e.copy(rb_v[tb2], lnC),
                 reads=["sg0"], writes=["sq0"], tag=f"rbf_p{p}t{tb2}")
            B.op("scalar", lambda e, tb2=tb2: e.copy(mrb_v[tb2], lnB),
                 reads=["g1"], writes=["sq1"], tag=f"mrbf_p{p}t{tb2}")
        for kc in range(NK):
            xsl = xn[:, kc * TP:(kc + 1) * TP]
            B.op("vector",
                 lambda e, xsl=xsl: e.tensor_mul(txn[:], xsl, rb_all),
                 reads=[f"xn{kc}", "sq0"], writes=["txn"],
                 tag=f"n1_p{p}k{kc}")
            B.op("vector",
                 lambda e, xsl=xsl: e.tensor_sub(xsl, txn[:], mrb_all),
                 reads=["txn", "sq1"], writes=[f"xn{kc}"],
                 tag=f"n2_p{p}k{kc}")

    def plan_p2(p):
        for tb2 in range(2):
            zfns, zrd, lfns, lrd = [], [], [], []
            for kc in range(NK):
                def zf(e, kc=kc, tb2=tb2):
                    return e.matmul(
                        pb[tb2][:], lA[:, kc * 128:(kc + 1) * 128],
                        xn[:, kc * TP + tb2 * TB: kc * TP + (tb2 + 1) * TB],
                        start=(kc == 0), stop=(kc == NK - 1))
                zfns.append(zf)
                zrd.append(["lA", f"xn{kc}"])

                def lf(e, kc=kc, tb2=tb2):
                    return e.matmul(
                        pb[2 + tb2][0:16, :], lR[:, kc * 16:(kc + 1) * 16],
                        xn[:, kc * TP + tb2 * TB: kc * TP + (tb2 + 1) * TB],
                        start=(kc == 0), stop=(kc == NK - 1))
                lfns.append(lf)
                lrd.append(["lR", f"xn{kc}"])
            B.acc_group("tensor", zfns, zrd, [f"pb{tb2}"], inc=("pe", 1),
                        tag=f"mmz_p{p}t{tb2}")
            B.acc_group("tensor", lfns, lrd, [f"pb{2 + tb2}"], inc=("pe", 1),
                        tag=f"mmlg_p{p}t{tb2}")
            B.op("vector",
                 lambda e, tb2=tb2: e.tensor_copy(
                     z_sb[:, tb2 * TB:(tb2 + 1) * TB], pb[tb2][:]),
                 reads=[f"pb{tb2}"], writes=["z_sb"], tag=f"evz_p{p}t{tb2}")


    def plan_p2_alpha(p, tb2):
        bk = [(4, 5, 6, 7), (0, 1, 2, 3)][tb2]
        srw = srow if tb2 == 0 else qrow
        srn = "srow" if tb2 == 0 else "qrow"
        B.op("vector",
             lambda e, tb2=tb2: e.tensor_copy(lg_tmp2[tb2][:],
                                              pb[2 + tb2][0:16, :]),
             reads=[f"pb{2 + tb2}"], writes=["lg"], tag=f"evlg_p{p}t{tb2}")
        B.op("tensor",
             lambda e, tb2=tb2, bk=bk: e.matmul(pb[bk[0]][0:4, :], st_sb[:],
                                                lg_tmp2[tb2][:],
                                                start=True, stop=True),
             reads=["st", "lg"], writes=[f"pb{bk[0]}"], inc=("pe", 1),
             tag=f"mmlsel_p{p}t{tb2}")
        B.op("scalar",
             lambda e, tb2=tb2, bk=bk: e.activation(esb2[tb2][:],
                                                    pb[bk[0]][0:4, :], AF.Exp,
                                                    bias=ce_sb[:], scale=1.0),
             reads=[f"pb{bk[0]}", "ce"], writes=[f"esb{tb2}"],
             tag=f"exp_p{p}t{tb2}")
        B.op("tensor",
             lambda e, tb2=tb2, bk=bk: e.matmul(pb[bk[1]][0:1, :], ones4[:],
                                                esb2[tb2][:],
                                                start=True, stop=True),
             reads=[f"esb{tb2}"], writes=[f"pb{bk[1]}"], inc=("pe", 1),
             tag=f"mmss_p{p}t{tb2}")
        B.op("vector",
             lambda e, bk=bk, srw=srw: e.tensor_copy(srw[0:1, :],
                                                     pb[bk[1]][0:1, :]),
             reads=[f"pb{bk[1]}"], writes=[srn], tag=f"evss_p{p}t{tb2}")
        B.op("vector", lambda e, srw=srw: e.reciprocal(srw[0:1, :], srw[0:1, :]),
             reads=[srn], writes=[srn], tag=f"rs_p{p}t{tb2}")
        B.op("tensor",
             lambda e, bk=bk, srw=srw: e.matmul(pb[bk[2]][:], ones1[:],
                                                srw[0:1, :],
                                                start=True, stop=True),
             reads=[srn], writes=[f"pb{bk[2]}"], inc=("pe", 1),
             tag=f"mmrsb_p{p}t{tb2}")
        B.op("vector",
             lambda e, tb2=tb2, bk=bk: e.tensor_copy(rs_bf[tb2][:],
                                                     pb[bk[2]][:]),
             reads=[f"pb{bk[2]}"], writes=[f"rsbf{tb2}"],
             tag=f"evrs_p{p}t{tb2}")
        B.op("tensor",
             lambda e, tb2=tb2, bk=bk: e.matmul(pb[bk[3]][:], m2f[:],
                                                esb2[tb2][:],
                                                start=True, stop=True),
             reads=["m2f", f"esb{tb2}"], writes=[f"pb{bk[3]}"], inc=("pe", 1),
             tag=f"mmau_p{p}t{tb2}")
        B.op("vector",
             lambda e, tb2=tb2, bk=bk: e.tensor_scalar_mul(aexp2[tb2][:],
                                                           pb[bk[3]][:],
                                                           SCALING),
             reads=[f"pb{bk[3]}"], writes=[f"aexp{tb2}"],
             tag=f"au2_p{p}t{tb2}")
        txv = txn[:, tb2 * TB:(tb2 + 1) * TB]
        B.op("vector",
             lambda e, tb2=tb2, txv=txv: e.tensor_mul(
                 txv, z_sb[:, tb2 * TB:(tb2 + 1) * TB], aexp2[tb2][:]),
             reads=["z_sb", f"aexp{tb2}"], writes=["txn"],
             tag=f"zw1_p{p}t{tb2}")
        B.op("vector",
             lambda e, tb2=tb2, txv=txv: e.tensor_mul(
                 zw[:, tb2 * TB:(tb2 + 1) * TB], txv, rs_bf[tb2][:]),
             reads=["txn", f"rsbf{tb2}"], writes=[f"zw{tb2}"],
             tag=f"zw2_p{p}t{tb2}")

    def plan_iter(p, i):
        par = i % 2
        base = par * 4
        mini_bank = (1 - par) * 4 + 3
        banding = (p == 0)

        if i + 1 < NN:
            sl = (i + 1) % 2
            dma(f"sx{sl}", xbuf[sl][:],
                d_xt[(i + 1) * 128:(i + 2) * 128, p * TP:(p + 1) * TP],
                writes=[f"xb{sl}"], tag=f"xr_p{p}i{i + 1}")
            w_loads_half(p, i + 1, 0)

        if banding and 3 <= i + 3 <= 31:
            band_mask_half(i + 3, 0)

        for tb in range(2):
            sbank = base + tb
            ybank = base + 2 + tb
            fns, rds = [], []
            for kc in range(NK):
                def f(e, kc=kc, tb=tb, sbank=sbank, i=i):
                    return e.matmul(
                        pb[sbank][:], wm[i % 2][:, kc * 128:(kc + 1) * 128],
                        xn[:, kc * TP + tb * TB: kc * TP + (tb + 1) * TB],
                        start=(kc == 0), stop=(kc == NK - 1))
                fns.append(f)
                rds.append([f"wm{i % 2}", f"xn{kc}"])
            B.acc_group("tensor", fns, rds, [f"pb{sbank}"], inc=("pe", 1),
                        tag=f"mms_p{p}i{i}t{tb}")

            if tb == 0:
                if i + 1 < NN:
                    wm_mults_half(p, i + 1, 0)
                    wb_casts_half(p, i + 1, 0)
                    w_loads_half(p, i + 1, 1)

            fns, rds = [], []
            for kc in range(NK):
                def f(e, kc=kc, tb=tb, ybank=ybank, i=i):
                    return e.matmul(
                        pb[ybank][:], wb[i % 2][:, kc * 128:(kc + 1) * 128],
                        xn[:, kc * TP + tb * TB: kc * TP + (tb + 1) * TB],
                        start=(kc == 0), stop=False)
                fns.append(f)
                rds.append([f"wb{i % 2}", f"xn{kc}"])

            def fB(e, tb=tb, ybank=ybank, i=i):
                return e.matmul(pb[ybank][:], lB[:, i * 128:(i + 1) * 128],
                                zw[:, tb * TB:(tb + 1) * TB],
                                start=False, stop=True)
            fns.append(fB)
            rds.append(["lB", f"zw{tb}"])
            B.acc_group("tensor", fns, rds, [f"pb{ybank}"], inc=("pe", 1),
                        tag=f"mmy_p{p}i{i}t{tb}")

            if tb == 1 and i + 1 < NN:
                wm_mults_half(p, i + 1, 1)
                wb_casts_half(p, i + 1, 1)

            B.op("scalar",
                 lambda e, tb=tb, ybank=ybank, i=i: e.activation(
                     g_sb[:, tb * TB:(tb + 1) * TB], pb[ybank][:], AF.Silu,
                     bias=b_sb[:, i:i + 1], scale=1.0),
                 reads=[f"pb{ybank}", "b_sb"], writes=[f"g{tb}"],
                 tag=f"silu_p{p}i{i}t{tb}")
            B.op("vector",
                 lambda e, tb=tb, sbank=sbank: e.tensor_mul(
                     sg_sb[:, tb * TB:(tb + 1) * TB], pb[sbank][:],
                     g_sb[:, tb * TB:(tb + 1) * TB]),
                 reads=[f"pb{sbank}", f"g{tb}"], writes=[f"sg{tb}"],
                 tag=f"sg_p{p}i{i}t{tb}")
            B.op("gpsimd",
                 lambda e, tb=tb, i=i: e.tensor_tensor(
                     out_sb[i % 2][:, tb * TB:(tb + 1) * TB],
                     sg_sb[:, tb * TB:(tb + 1) * TB],
                     xbuf[i % 2][:, tb * TB:(tb + 1) * TB], ALU.add),
                 reads=[f"sg{tb}", f"xb{i % 2}"], writes=[f"o{i % 2}"],
                 tag=f"res_p{p}i{i}t{tb}")

        gdma(f"so{i % 2}", d_out[i * 128:(i + 1) * 128, p * TP:(p + 1) * TP],
             out_sb[i % 2][:], reads=[f"o{i % 2}"], tag=f"st_p{p}i{i}")

    plan_startup()
    for p in range(NPASS):
        plan_ln(p)
        plan_p2(p)
        w_prep_serial(p, 0)
        plan_p2_alpha(p, 0)
        plan_p2_alpha(p, 1)
        dma("sx0", xbuf[0][:], d_xt[0:128, p * TP:(p + 1) * TP],
            writes=["xb0"], tag=f"xr_p{p}i0")
        for i in range(NN):
            plan_iter(p, i)

    B.finalize()
    fin = []
    for s in ("so0", "so1", "sbb0", "sbb1", "sm0", "sm1"):
        if s in B.counts:
            fin.append(B.ev_at(s, B.counts[s]))
    B.ops["gpsimd"].append(Op("gpsimd", "WAIT_ONLY", fin, None, "final_wait"))
    B.validate()

    semh = {s: E(nc.semaphore(s)) for s in sorted(B.semnames)}
    blk = E(nc.Block())

    def emitter(name):
        def run(eng):
            hw = {}
            own = B.ENG_SEM.get(name)
            for o in B.ops[name]:
                for w in o.waits:
                    if hw.get(w.sem, -1) >= w.count:
                        continue
                    eng.wait_ge(semh[w.sem], w.count)
                    hw[w.sem] = w.count
                if o.fn == "NOP_INC":
                    if name == "tensor":
                        ins = eng.matmul(pb[0][0:1, 0:1], ones1[0:1, 0:1],
                                         ones1[0:1, 0:1], start=True, stop=True)
                    elif name == "gpsimd":
                        ins = eng.memset(sgn_t[0:1, 0:1], 0.0)
                    elif name == "scalar":
                        ins = eng.copy(sgn_t[0:1, 0:1], sgn_t[0:1, 0:1])
                    else:
                        ins = eng.tensor_copy(sgn_t[0:1, 0:1], sgn_t[0:1, 0:1])
                    ins.then_inc(semh[o.inc[0]], o.inc[1])
                    continue
                if o.fn == "WAIT_ONLY":
                    continue
                ins = o.fn(eng)
                if o.inc is not None:
                    ins.then_inc(semh[o.inc[0]], o.inc[1])
        return run

    blk.sync(emitter("sync"))
    blk.tensor(emitter("tensor"))
    blk.vector(emitter("vector"))
    blk.scalar(emitter("scalar"))
    blk.gpsimd(emitter("gpsimd"))
    ctx.close()
    return nc


def kernel(L0_attn_data, L0_attn_weight, L0_attn_mask, L0_ffn_W_base,
           L0_ffn_b_base, L0_ffn_lora_A, L0_ffn_lora_B, L0_ffn_router_W,
           L0_ffn_router_b, L0_ffn_domain_prior, L0_ffn_active):
    x = np.ascontiguousarray(np.asarray(L0_attn_data, np.float32)
                             .reshape(16384, D))
    W = np.asarray(L0_attn_weight, np.float32)
    mask = np.asarray(L0_attn_mask, np.int32)
    Wb = np.asarray(L0_ffn_W_base, np.float32)
    bbase = np.asarray(L0_ffn_b_base, np.float32)
    lora_A = np.asarray(L0_ffn_lora_A, np.float32)
    lora_B = np.asarray(L0_ffn_lora_B, np.float32)
    rW = np.asarray(L0_ffn_router_W, np.float32)
    rb = np.asarray(L0_ffn_router_b, np.float32)
    prior = np.asarray(L0_ffn_domain_prior, np.float32)
    active = np.asarray(L0_ffn_active, np.int32)

    def prep2(M):
        return np.ascontiguousarray(
            M.reshape(NN, 128, NK, 128).transpose(0, 3, 2, 1)
            .reshape(NN, 128, D))

    wp = prep2(W)
    wbp = prep2(Wb)
    mkp = np.ascontiguousarray(
        mask.reshape(D, 4, 32, 32).transpose(0, 1, 3, 2).reshape(D, D))
    ap_ = np.ascontiguousarray(
        lora_A.transpose(2, 0, 1).reshape(D, 128).reshape(NK, 128, 128)
        .transpose(1, 0, 2).reshape(128, D))
    rp = np.ascontiguousarray(
        rW.T.reshape(NK, 128, 16).transpose(1, 0, 2).reshape(128, 512))
    bp = np.ascontiguousarray(lora_B.transpose(0, 2, 1).reshape(128, D))

    e4l = np.zeros((4, 128), BF)
    for j in range(4):
        e4l[j, j * 32:(j + 1) * 32] = 1
    e4p = np.ascontiguousarray(e4l.T)
    ekron = np.zeros((16, 128), np.float32)
    for Ei in range(16):
        ekron[Ei, Ei * 8:(Ei + 1) * 8] = 1

    common = dict(
        wp=wp, wbp=wbp, mkp=mkp, ap_=ap_, rp=rp, bp=bp,
        bb=np.ascontiguousarray(bbase.reshape(32, 128)),
        rbv=np.ascontiguousarray(rb.reshape(16, 1)),
        pri=np.ascontiguousarray(prior.reshape(16, 1)),
        actv=np.ascontiguousarray(active.reshape(1, 4)),
        ident=np.eye(32, dtype=np.float32),
        ones1f=np.ones((1, 128), np.float32),
        onesbf=np.ones((128, 1), BF),
        e4l=e4l, e4p=e4p, ekron=ekron,
        iota16=np.arange(16, dtype=np.float32).reshape(1, 16),
        ones4f=np.ones((4, 1), np.float32),
    )
    in_maps = []
    for c in range(NCORE):
        m = dict(common)
        m["xt"] = np.ascontiguousarray(x[c * TPC:(c + 1) * TPC].T)
        in_maps.append(m)

    nc = build_program()
    res = run_bass_kernel_spmd(nc, in_maps, core_ids=list(range(NCORE)))
    out = np.concatenate([np.asarray(r["out"]) for r in res.results], axis=1)
    return np.ascontiguousarray(out.T).reshape(2, 8192, D).astype(np.float32)


if __name__ == "__main__":
    build_program()
    print("build ok")


# version 24
# speedup vs baseline: 1.2208x; 1.0299x over previous
import numpy as np
import ml_dtypes
import concourse.bass as bass
import concourse.mybir as mybir
from concourse.bass_utils import run_bass_kernel_spmd

F32 = mybir.dt.float32
BF16 = mybir.dt.bfloat16
I32 = mybir.dt.int32
AF = mybir.ActivationFunctionType
ALU = mybir.AluOpType
BF = ml_dtypes.bfloat16

NCORE = 8
D = 4096
NK = 32
NN = 32
TPC = 2048
TP = 1024
NPASS = 2
TB = 512
EPS_LN = 1e-5
EPS_PRIOR = 1e-12
SCALING = 2.0


class Ev:
    __slots__ = ("sem", "count")

    def __init__(self, sem, count=None):
        self.sem = sem
        self.count = count


class Res:
    __slots__ = ("last_w", "readers", "name")

    def __init__(self, name):
        self.name = name
        self.last_w = None
        self.readers = []


class Op:
    __slots__ = ("eng", "fn", "waits", "inc", "tag")

    def __init__(self, eng, fn, waits, inc, tag):
        self.eng, self.fn, self.waits, self.inc, self.tag = \
            eng, fn, waits, inc, tag


class Builder:
    ENGS = ("sync", "tensor", "vector", "scalar", "gpsimd")
    ENG_SEM = {"tensor": "pe", "vector": "dve", "scalar": "act",
               "gpsimd": "pool"}

    def __init__(self):
        self.ops = {e: [] for e in self.ENGS}
        self.counts = {}
        self.pending = {e: [] for e in self.ENGS}
        self.res = {}
        self.semnames = set()

    def r(self, name):
        if name not in self.res:
            self.res[name] = Res(name)
        return self.res[name]

    def sem(self, name):
        self.semnames.add(name)
        self.counts.setdefault(name, 0)
        return name

    def ev_at(self, semname, count):
        return Ev(self.sem(semname), count)

    def _mkop(self, eng, fn, wl, inc, tag):
        if inc is not None:
            semname, amt = inc
            self.sem(semname)
            self.counts[semname] += amt
            ev = Ev(semname, self.counts[semname])
            if semname == self.ENG_SEM.get(eng):
                for p in self.pending[eng]:
                    p.count = self.counts[semname]
                self.pending[eng] = []
        else:
            ev = Ev(self.sem(self.ENG_SEM[eng]))
            self.pending[eng].append(ev)
        self.ops[eng].append(Op(eng, fn, wl, inc, tag))
        return ev

    def op(self, eng, fn, reads=(), writes=(), waits=(), inc=None, tag=""):
        if inc is None and eng in ("vector", "scalar", "gpsimd"):
            inc = (self.ENG_SEM[eng], 1)
        reads = [self.r(x) if isinstance(x, str) else x for x in reads]
        writes = [self.r(x) if isinstance(x, str) else x for x in writes]
        wl = list(waits)
        for rs in reads:
            if rs.last_w is not None:
                wl.append(rs.last_w)
        for rs in writes:
            if rs.last_w is not None:
                wl.append(rs.last_w)
            wl.extend(rs.readers)
        ev = self._mkop(eng, fn, wl, inc, tag)
        for rs in reads:
            rs.readers.append(ev)
        for rs in writes:
            rs.last_w = ev
            rs.readers = []
        return ev

    def acc_group(self, eng, fns, reads_each, writes, waits=(), inc=None,
                  tag=""):
        writes = [self.r(x) if isinstance(x, str) else x for x in writes]
        wl0 = list(waits)
        for rs in writes:
            if rs.last_w is not None:
                wl0.append(rs.last_w)
            wl0.extend(rs.readers)
        n = len(fns)
        for j, fn in enumerate(fns):
            reads = [self.r(x) if isinstance(x, str) else x
                     for x in reads_each[j]]
            wl = list(wl0) if j == 0 else []
            for rs in reads:
                if rs.last_w is not None:
                    wl.append(rs.last_w)
            ev = self._mkop(eng, fn, wl, inc if j == n - 1 else None,
                            f"{tag}[{j}]")
            for rs in reads:
                rs.readers.append(ev)
        for rs in writes:
            rs.last_w = ev
            rs.readers = []
        return ev

    def finalize(self):
        for eng in ("tensor", "vector", "scalar", "gpsimd"):
            if self.pending[eng]:
                semn = self.sem(self.ENG_SEM[eng])
                self.counts[semn] += 1
                for p in self.pending[eng]:
                    p.count = self.counts[semn]
                self.pending[eng] = []
                self.ops[eng].append(Op(eng, "NOP_INC", [], (semn, 1),
                                        "final_resolve"))

    def validate(self):
        for e in self.ENGS:
            for o in self.ops[e]:
                for w in o.waits:
                    assert w.count is not None, \
                        f"unresolved wait in {e}:{o.tag} (sem {w.sem})"
        ptr = {e: 0 for e in self.ENGS}
        cnt = {s: 0 for s in self.semnames}
        total = sum(len(v) for v in self.ops.values())
        done = 0
        progress = True
        while progress:
            progress = False
            for e in self.ENGS:
                own = self.ENG_SEM.get(e)
                while ptr[e] < len(self.ops[e]):
                    o = self.ops[e][ptr[e]]
                    if any(cnt[w.sem] < w.count for w in o.waits):
                        break
                    if o.inc is not None:
                        cnt[o.inc[0]] += o.inc[1]
                    ptr[e] += 1
                    done += 1
                    progress = True
        if done != total:
            msg = []
            for e in self.ENGS:
                if ptr[e] < len(self.ops[e]):
                    o = self.ops[e][ptr[e]]
                    bad = [(w.sem, w.count, cnt[w.sem]) for w in o.waits
                           if cnt[w.sem] < w.count]
                    msg.append(f"{e}@{ptr[e]}/{len(self.ops[e])} "
                               f"tag={o.tag} missing={bad}")
            raise RuntimeError("schedule deadlock:\n" + "\n".join(msg))


def build_program():
    nc = bass.Bass("TRN2", target_bir_lowering=False)
    B = Builder()

    d_xt = nc.dram_tensor("xt", [D, TPC], F32, kind="ExternalInput").ap()
    d_wp = nc.dram_tensor("wp", [NN, 128, D], F32, kind="ExternalInput").ap()
    d_wbp = nc.dram_tensor("wbp", [NN, 128, D], F32, kind="ExternalInput").ap()
    d_mk = nc.dram_tensor("mkp", [D, D], I32, kind="ExternalInput").ap()
    d_ap = nc.dram_tensor("ap_", [128, D], F32, kind="ExternalInput").ap()
    d_rp = nc.dram_tensor("rp", [128, 512], F32, kind="ExternalInput").ap()
    d_bp = nc.dram_tensor("bp", [128, D], F32, kind="ExternalInput").ap()
    d_bb = nc.dram_tensor("bb", [32, 128], F32, kind="ExternalInput").ap()
    d_rbv = nc.dram_tensor("rbv", [16, 1], F32, kind="ExternalInput").ap()
    d_pri = nc.dram_tensor("pri", [16, 1], F32, kind="ExternalInput").ap()
    d_act = nc.dram_tensor("actv", [1, 4], I32, kind="ExternalInput").ap()
    d_ident = nc.dram_tensor("ident", [32, 32], F32, kind="ExternalInput").ap()
    d_ones1 = nc.dram_tensor("ones1f", [1, 128], F32, kind="ExternalInput").ap()
    d_onesb = nc.dram_tensor("onesbf", [128, 1], BF16, kind="ExternalInput").ap()
    d_e4l = nc.dram_tensor("e4l", [4, 128], BF16, kind="ExternalInput").ap()
    d_e4p = nc.dram_tensor("e4p", [128, 4], BF16, kind="ExternalInput").ap()
    d_ekron = nc.dram_tensor("ekron", [16, 128], F32, kind="ExternalInput").ap()
    d_iota = nc.dram_tensor("iota16", [1, 16], F32, kind="ExternalInput").ap()
    d_ones4 = nc.dram_tensor("ones4f", [4, 1], F32, kind="ExternalInput").ap()
    d_out = nc.dram_tensor("out", [D, TPC], F32, kind="ExternalOutput").ap()
    d_scr2 = nc.dram_tensor("scr2", [128, 128], BF16, kind="Internal").ap()
    from contextlib import ExitStack
    ctx = ExitStack()
    E = ctx.enter_context

    xn = E(nc.sbuf_tensor("xn", [128, NK * TP], BF16))
    wstage = [E(nc.sbuf_tensor(f"wstage{j}", [128, 1024], F32)) for j in range(2)]
    wbstage = [E(nc.sbuf_tensor(f"wbstage{j}", [128, 1024], F32)) for j in range(2)]
    mstage = [E(nc.sbuf_tensor(f"mstage{j}", [128, 1024], I32)) for j in range(3)]
    wm = [E(nc.sbuf_tensor(f"wm{j}", [128, D], BF16)) for j in range(2)]
    wb = [E(nc.sbuf_tensor(f"wb{j}", [128, D], BF16)) for j in range(2)]
    mcB = E(nc.sbuf_tensor("mcB", [128, 512], BF16))
    mcC = E(nc.sbuf_tensor("mcC", [128, 256], BF16))
    pooled = E(nc.sbuf_tensor("pooled", [128, 128], BF16))
    bm01a = E(nc.sbuf_tensor("bm01a", [128, 128], BF16))
    sgn_t = E(nc.sbuf_tensor("sgn_t", [128, 4], F32))
    sgn_t2 = E(nc.sbuf_tensor("sgn_t2", [128, 128], BF16))
    bmra = E(nc.sbuf_tensor("bmra", [4, 2048], BF16))
    bmtexp = E(nc.sbuf_tensor("bmtexp", [128, D], BF16))
    xbuf = [E(nc.sbuf_tensor(f"xbuf{j}", [128, TP], F32)) for j in range(2)]
    out_sb = [E(nc.sbuf_tensor(f"out_sb{j}", [128, TP], F32)) for j in range(2)]
    lA = E(nc.sbuf_tensor("lA", [128, D], BF16))
    lR = E(nc.sbuf_tensor("lR", [128, 512], BF16))
    lB = E(nc.sbuf_tensor("lB", [128, D], BF16))
    z_sb = E(nc.sbuf_tensor("z_sb", [128, TP], BF16))
    zw = E(nc.sbuf_tensor("zw", [128, TP], BF16))
    txn = E(nc.sbuf_tensor("txn", [128, TP], BF16))
    sq = E(nc.sbuf_tensor("sq", [128, 2048], BF16))
    aexp2 = [E(nc.sbuf_tensor(f"aexp2_{j}", [128, TB], BF16)) for j in range(2)]
    rs_bf = [E(nc.sbuf_tensor(f"rs_bf{j}", [128, TB], BF16)) for j in range(2)]
    _lg = E(nc.sbuf_tensor("lg_tmp", [16, TB], F32))
    lg_tmp2 = [_lg, _lg]
    esb2 = [E(nc.sbuf_tensor(f"esb{j}", [4, TB], F32)) for j in range(2)]
    srow = E(nc.sbuf_tensor("srow", [1, TB], F32))
    qrow = E(nc.sbuf_tensor("qrow", [1, TB], F32))
    g_sb = E(nc.sbuf_tensor("g_sb", [128, TP], F32))
    sg_sb = E(nc.sbuf_tensor("sg_sb", [128, TP], F32))
    b_sb = E(nc.sbuf_tensor("b_sb", [128, 32], F32))
    bb_sb = E(nc.sbuf_tensor("bb_sb", [32, 128], F32))
    ident = E(nc.sbuf_tensor("ident_sb", [32, 32], F32))
    ones1 = E(nc.sbuf_tensor("ones1_sb", [1, 128], F32))
    onesb = E(nc.sbuf_tensor("onesb_sb", [128, 1], BF16))
    e4l = E(nc.sbuf_tensor("e4l_sb", [4, 128], BF16))
    e4p = E(nc.sbuf_tensor("e4p_sb", [128, 4], BF16))
    ekron = E(nc.sbuf_tensor("ekron_sb", [16, 128], F32))
    iota = E(nc.sbuf_tensor("iota_sb", [1, 16], F32))
    ones4 = E(nc.sbuf_tensor("ones4_sb", [4, 1], F32))
    actv = E(nc.sbuf_tensor("actv_sb", [1, 4], I32))
    rbv = E(nc.sbuf_tensor("rbv_sb", [16, 1], F32))
    pri = E(nc.sbuf_tensor("pri_sb", [16, 1], F32))
    a_f = E(nc.sbuf_tensor("a_f", [1, 4], F32))
    a_cl = E(nc.sbuf_tensor("a_cl", [1, 4], F32))
    a_col = E(nc.sbuf_tensor("a_col", [4, 1], F32))
    dt_sb = E(nc.sbuf_tensor("dt_sb", [4, 16], F32))
    ab_sb = E(nc.sbuf_tensor("ab_sb", [4, 16], F32))
    s_sb = E(nc.sbuf_tensor("s_sb", [4, 16], F32))
    st_sb = E(nc.sbuf_tensor("st_sb", [16, 4], F32))
    rbse = E(nc.sbuf_tensor("rbse", [4, 1], F32))
    rbse2 = E(nc.sbuf_tensor("rbse2", [4, 1], F32))
    lp_sb = E(nc.sbuf_tensor("lp_sb", [4, 1], F32))
    ce_sb = E(nc.sbuf_tensor("ce_sb", [4, 1], F32))
    m2f = E(nc.sbuf_tensor("m2f", [4, 128], F32))
    pb = [E(nc.psum_tensor(f"pb{j}", [128, 512], F32)) for j in range(8)]

    lnA = g_sb[:, 0:512]
    lnB = g_sb[:, 512:1024]
    lnC = sg_sb[:, 0:512]
    lnD = sg_sb[:, 512:1024]
    rb_v = [sq[:, 0:512], sq[:, 512:1024]]
    mrb_v = [sq[:, 1024:1536], sq[:, 1536:2048]]
    rb_all = sq[:, 0:1024]
    mrb_all = sq[:, 1024:2048]

    def dma(semn, dst, src, reads=(), writes=(), waits=(), tag=""):
        return B.op("sync", (lambda e, d=dst, s=src: e.dma_start(out=d, in_=s)),
                    reads=reads, writes=writes, waits=waits,
                    inc=(semn, 16), tag=tag)

    def adma(semn, dst, src, reads=(), writes=(), waits=(), tag=""):
        return B.op("scalar", (lambda e, d=dst, s=src: e.dma_start(out=d, in_=s)),
                    reads=reads, writes=writes, waits=waits,
                    inc=(semn, 16), tag=tag)

    def gdma(semn, dst, src, reads=(), writes=(), waits=(), tag="", slow=False):
        return B.op("gpsimd",
                    (lambda e, d=dst, s=src, sl=slow: e.dma_start(
                        out=d, in_=s, allow_slow_non_contiguous=sl)),
                    reads=reads, writes=writes, waits=waits,
                    inc=(semn, 16), tag=tag)

    def plan_startup():
        for dst, src, nm in [
            (ident[:], d_ident, "ident"), (ones1[:], d_ones1, "ones1"),
            (onesb[:], d_onesb, "onesb"), (e4l[:], d_e4l, "e4l"),
            (e4p[:], d_e4p, "e4p"), (ekron[:], d_ekron, "ekron"),
            (iota[:], d_iota, "iota"), (ones4[:], d_ones4, "ones4"),
            (actv[:], d_act, "actv"), (rbv[:], d_rbv, "rbv"),
            (pri[:], d_pri, "pri"), (bb_sb[:], d_bb, "bb"),
        ]:
            dma("init", dst, src, tag=f"ld_{nm}")
        for q in range(4):
            sl = q % 2
            dma(f"sw{sl}", wstage[sl][:], d_ap[:, q * 1024:(q + 1) * 1024],
                writes=[f"wst{sl}"], tag=f"ld_lA{q}")
            B.op("scalar",
                 lambda e, q=q, sl=sl: e.copy(lA[:, q * 1024:(q + 1) * 1024],
                                              wstage[sl][:]),
                 reads=[f"wst{sl}"], writes=["lA"], tag=f"c_lA{q}")
        for q in range(4):
            sl = q % 2
            dma(f"swb{sl}", wbstage[sl][:], d_bp[:, q * 1024:(q + 1) * 1024],
                writes=[f"wbst{sl}"], tag=f"ld_lB{q}")
            B.op("scalar",
                 lambda e, q=q, sl=sl: e.copy(lB[:, q * 1024:(q + 1) * 1024],
                                              wbstage[sl][:]),
                 reads=[f"wbst{sl}"], writes=["lB"], tag=f"c_lB{q}")
        dma("sw0", wstage[0][:, 0:512], d_rp, writes=["wst0"], tag="ld_lR")
        B.op("scalar", lambda e: e.copy(lR[:], wstage[0][:, 0:512]),
             reads=["wst0"], writes=["lR"], tag="c_lR")

        winit = [B.ev_at("init", B.counts.get("init", 0))]

        B.op("scalar", lambda e: e.copy(a_f[:], actv[:]),
             waits=winit, writes=["a_f"], tag="cast_active")
        B.op("vector",
             lambda e: e.tensor_scalar(a_cl[:], a_f[:], 0.0, 15.0,
                                       ALU.max, ALU.min),
             reads=["a_f"], writes=["a_cl"], tag="clamp_active")
        B.op("tensor",
             lambda e: e.transpose(pb[4][0:4, 0:1], a_cl[:], ident[0:1, 0:1]),
             waits=winit, reads=["a_cl"], writes=["pb4"], inc=("pe", 1),
             tag="tr_acol")
        B.op("vector", lambda e: e.tensor_copy(a_col[:], pb[4][0:4, 0:1]),
             reads=["pb4"], writes=["a_col"], tag="ev_acol")
        B.op("tensor",
             lambda e: e.matmul(pb[5][0:4, 0:16], ones1[0:1, 0:4], iota[:],
                                start=True, stop=True),
             waits=winit, writes=["pb5"], inc=("pe", 1), tag="mm_iotab")
        B.op("vector",
             lambda e: e.tensor_single_scalar(dt_sb[:], pb[5][0:4, 0:16],
                                              a_col[:], ALU.subtract),
             reads=["pb5", "a_col"], writes=["dt"], tag="d_sub")
        B.op("scalar", lambda e: e.activation(ab_sb[:], dt_sb[:], AF.Abs),
             reads=["dt"], writes=["absd"], tag="absd")
        B.op("vector",
             lambda e: e.tensor_scalar(dt_sb[:], ab_sb[:], -1.0, 1.0,
                                       ALU.mult, ALU.add),
             reads=["absd"], writes=["dt"], tag="S_lin")
        B.op("vector",
             lambda e: e.tensor_scalar_max(s_sb[:], dt_sb[:], 0.0),
             reads=["dt"], writes=["S"], tag="S_onehot")
        B.op("tensor",
             lambda e: e.transpose(pb[6][0:16, 0:4], s_sb[:], ident[0:4, 0:4]),
             reads=["S"], writes=["pb6"], inc=("pe", 1), tag="tr_S")
        B.op("vector", lambda e: e.tensor_copy(st_sb[:], pb[6][0:16, 0:4]),
             reads=["pb6"], writes=["st"], tag="ev_st")
        B.op("tensor",
             lambda e: e.matmul(pb[7][0:4, 0:1], st_sb[:], rbv[:],
                                start=True, stop=True),
             waits=winit, reads=["st"], writes=["pb7"], inc=("pe", 1),
             tag="mm_rbsel")
        B.op("tensor",
             lambda e: e.matmul(pb[7][0:4, 1:2], st_sb[:], pri[:],
                                start=True, stop=True),
             reads=["st"], writes=["pb7"], inc=("pe", 1), tag="mm_prisel")
        B.op("vector", lambda e: e.tensor_copy(rbse[:], pb[7][0:4, 0:1]),
             reads=["pb7"], writes=["rbse"], tag="ev_rbse")
        B.op("vector",
             lambda e: e.tensor_scalar_add(rbse2[:], pb[7][0:4, 1:2],
                                           EPS_PRIOR),
             reads=["pb7"], writes=["prieps"], tag="pri_eps")
        B.op("scalar",
             lambda e: e.activation(lp_sb[:], rbse2[:], AF.Ln),
             reads=["prieps"], writes=["lp"], tag="ln_prior")
        B.op("vector", lambda e: e.tensor_add(ce_sb[:], rbse[:], lp_sb[:]),
             reads=["rbse", "lp"], writes=["ce"], tag="ce")
        B.op("tensor",
             lambda e: e.matmul(pb[4][0:4, 0:128], st_sb[:], ekron[:],
                                start=True, stop=True),
             reads=["st"], writes=["pb4"], inc=("pe", 1), tag="mm_M2")
        B.op("vector", lambda e: e.tensor_copy(m2f[:], pb[4][0:4, 0:128]),
             reads=["pb4"], writes=["m2f"], tag="ev_M2")
        B.op("tensor",
             lambda e: e.transpose(pb[5][:, 0:32], bb_sb[:], ident[0:32, 0:32]),
             waits=winit, writes=["pb5"], inc=("pe", 1), tag="tr_bb")
        B.op("vector", lambda e: e.tensor_copy(b_sb[:], pb[5][:, 0:32]),
             reads=["pb5"], writes=["b_sb"], tag="ev_bb")

    def band_mask_half(b, h):
        for q in (2 * h, 2 * h + 1):
            sl = (b * 4 + q) % 3
            dmaf = dma if h == 0 else adma
            dmaf(f"sm{sl}", mstage[sl][:],
                 d_mk[b * 128:(b + 1) * 128, q * 1024:(q + 1) * 1024],
                 writes=[f"mst{sl}"], tag=f"mkq_b{b}q{q}")
            B.op("vector",
                 lambda e, sl=sl: e.tensor_add(mcB[:], mstage[sl][:, 0:512],
                                               mstage[sl][:, 512:1024]),
                 reads=[f"mst{sl}"], writes=["mcB"], tag=f"h1_b{b}q{q}")

            def fred(e, q=q):
                iv = bass.AP(mcB.ap().tensor, 0, [[512, 128], [1, 32], [32, 16]])
                ov = pooled[:, q * 32:(q + 1) * 32].rearrange(
                    "p (r one) -> p r one", one=1)
                with nc.allow_low_precision(
                        reason="mask block sums are integers <=32, exact bf16"):
                    return e.tensor_reduce(ov, iv, mybir.AxisListType.X,
                                           ALU.add)
            B.op("vector", fred, reads=["mcB"], writes=["pooled"],
                 tag=f"hr_b{b}q{q}")

    def band_load_s2(b):
        band_mask_half(b, 0)
        band_mask_half(b, 1)

    def band_pooledmm(b, bank):
        B.op("tensor",
             lambda e, bank=bank: e.matmul(pb[bank][:, 128:132], pooled[:],
                                           e4p[:], start=True, stop=True),
             reads=["pooled"], writes=[f"pb{bank}"], inc=("pe", 1),
             tag=f"poolmm_b{b}")

    def band_pevac(b, bank):
        B.op("vector",
             lambda e, bank=bank, b=b: e.tensor_copy(
                 bm01a[:, b * 4:(b + 1) * 4], pb[bank][:, 128:132]),
             reads=[f"pb{bank}"], writes=["bm01a"], tag=f"pev_b{b}")

    def bounce_out_g(g):
        dma("sbb0", d_scr2[:, g * 32:(g + 1) * 32],
            bm01a[:, g * 32:(g + 1) * 32],
            reads=["bm01a"], writes=["scr2"], tag=f"bout_g{g}")

    def bounce_in_g(g):
        slot = g % 2
        dst = bmra[:, slot * 1024:(slot + 1) * 1024].rearrange(
            "j (kc x) -> j kc x", kc=32)
        src = d_scr2.rearrange("(kc j) x -> j kc x", j=4)[:, :,
                                                          g * 32:(g + 1) * 32]
        dma("sbb0", dst, src, reads=["scr2"], writes=["bmra"],
            tag=f"bin_g{g}")

    def band_e4l(b, bank):
        slot = (b // 8) % 2
        bl = b % 8

        def fn(e, bank=bank, slot=slot, bl=bl):
            rhs = bass.AP(bmra.ap().tensor, slot * 1024 + bl * 4,
                          [[2048, 4], [32, 32], [1, 4]])
            out = pb[bank][:, 0:128].rearrange("p (kc nb) -> p kc nb", nb=4)
            return e.matmul(out, e4l[:], rhs, start=True, stop=True)
        B.op("tensor", fn, reads=["bmra"], writes=[f"pb{bank}"],
             inc=("pe", 1), tag=f"e4l_b{b}")

    def band_texp_evac(b, bank):
        B.op("scalar",
             lambda e, bank=bank: e.activation(sgn_t2[:], pb[bank][:, 0:128],
                                               AF.Sign),
             reads=[f"pb{bank}"], writes=["sgn2"], tag=f"tsgn_b{b}")
        B.op("scalar",
             lambda e, b=b: e.activation(bmtexp[:, b * 128:(b + 1) * 128],
                                         sgn_t2[:], AF.Relu),
             reads=["sgn2"], writes=[f"texp{b}"], tag=f"texp_b{b}")

    def w_loads_half(p, i, h):
        for q in (2 * h, 2 * h + 1):
            sl = q % 2
            dma(f"sw{sl}", wstage[sl][:], d_wp[i][:, q * 1024:(q + 1) * 1024],
                writes=[f"wst{sl}"], tag=f"Wq_p{p}i{i}q{q}")
        for q in (2 * h, 2 * h + 1):
            sl = q % 2
            adma(f"swb{sl}", wbstage[sl][:],
                 d_wbp[i][:, q * 1024:(q + 1) * 1024],
                 writes=[f"wbst{sl}"], tag=f"WBq_p{p}i{i}q{q}")

    def wm_mults_half(p, i, h):
        dst = wm[i % 2]
        for q in (2 * h, 2 * h + 1):
            sl = q % 2

            def fn(e, q=q, sl=sl, dst=dst, i=i):
                o = dst[:, q * 1024:(q + 1) * 1024].rearrange(
                    "p (kc a b) -> p kc a b", a=4, b=32)
                i1 = wstage[sl][:].rearrange("p (kc a b) -> p kc a b",
                                             a=4, b=32)
                i2 = bass.AP(bmtexp.ap().tensor, i * 128 + q * 32,
                             [[D, 128], [4, 8], [1, 4], [0, 32]])
                return e.tensor_tensor(o, i1, i2, ALU.mult)
            B.op("vector", fn, reads=[f"wst{sl}", f"texp{i}"],
                 writes=[f"wm{i % 2}"], tag=f"wmmul_p{p}i{i}q{q}")

    def wb_casts_half(p, i, h):
        dst = wb[i % 2]
        for q in (2 * h, 2 * h + 1):
            sl = q % 2
            B.op("scalar",
                 lambda e, q=q, sl=sl, dst=dst: e.copy(
                     dst[:, q * 1024:(q + 1) * 1024], wbstage[sl][:]),
                 reads=[f"wbst{sl}"], writes=[f"wb{i % 2}"],
                 tag=f"wbcast_p{p}i{i}q{q}")

    def w_prep_serial(p, i):
        for h in range(2):
            w_loads_half(p, i, h)
            wm_mults_half(p, i, h)
            wb_casts_half(p, i, h)

    def plan_ln(p):
        for kc in range(NK):
            sl = kc % 2
            dma(f"sx{sl}", xbuf[sl][:],
                d_xt[kc * 128:(kc + 1) * 128, p * TP:(p + 1) * TP],
                writes=[f"xb{sl}"], tag=f"xt_p{p}kc{kc}")
            if p == 0 and kc <= 10:
                mb = 4 + 2 * (kc % 2)
                band_mask_half(kc, 0)
                band_mask_half(kc, 1)
                band_pooledmm(kc, mb)
                band_pevac(kc, mb)
            if p == 0 and kc == 9:
                bounce_out_g(0)
                bounce_in_g(0)
            if p == 0 and kc == 11:
                for bb in range(8):
                    band_e4l(bb, 4 + 2 * (bb % 2))
                    band_texp_evac(bb, 4 + 2 * (bb % 2))
            B.op("scalar",
                 lambda e, sl=sl, kc=kc: e.copy(xn[:, kc * TP:(kc + 1) * TP],
                                                xbuf[sl][:]),
                 reads=[f"xb{sl}"], writes=[f"xn{kc}"], tag=f"xc_p{p}kc{kc}")
            B.op("scalar",
                 lambda e, sl=sl: e.square(sq[:, sl * 1024:(sl + 1) * 1024],
                                           xbuf[sl][:]),
                 reads=[f"xb{sl}"], writes=[f"sq{sl}"], tag=f"xs_p{p}kc{kc}")
            for tb2 in range(2):
                wr = [f"pb{tb2}"] if kc in (0, NK - 1) else []
                B.op("tensor",
                     lambda e, kc=kc, tb2=tb2: e.matmul(
                         pb[tb2][0:1, :], onesb[:],
                         xn[:, kc * TP + tb2 * TB: kc * TP + (tb2 + 1) * TB],
                         start=(kc == 0), stop=(kc == NK - 1)),
                     reads=[f"xn{kc}"], writes=wr,
                     inc=("pe", 1),
                     tag=f"mmsum_p{p}k{kc}t{tb2}")
                wr = [f"pb{2 + tb2}"] if kc in (0, NK - 1) else []
                B.op("tensor",
                     lambda e, kc=kc, tb2=tb2, sl=sl: e.matmul(
                         pb[2 + tb2][0:1, :], onesb[:],
                         sq[:, sl * 1024 + tb2 * TB: sl * 1024 + (tb2 + 1) * TB],
                         start=(kc == 0), stop=(kc == NK - 1)),
                     reads=[f"sq{sl}"], writes=wr,
                     inc=("pe", 1),
                     tag=f"mmsq_p{p}k{kc}t{tb2}")

        for tb2 in range(2):
            B.op("vector",
                 lambda e, tb2=tb2: e.tensor_copy(srow[:], pb[tb2][0:1, :]),
                 reads=[f"pb{tb2}"], writes=["srow"], tag=f"evs_p{p}t{tb2}")
            B.op("vector",
                 lambda e, tb2=tb2: e.tensor_copy(qrow[:], pb[2 + tb2][0:1, :]),
                 reads=[f"pb{2 + tb2}"], writes=["qrow"], tag=f"evq_p{p}t{tb2}")
            mbk, qbk = 4 + 2 * tb2, 5 + 2 * tb2
            B.op("tensor",
                 lambda e, mbk=mbk: e.matmul(pb[mbk][:], ones1[:], srow[:],
                                             start=True, stop=True),
                 reads=["srow"], writes=[f"pb{mbk}"], inc=("pe", 1),
                 tag=f"mmbm_p{p}t{tb2}")
            B.op("tensor",
                 lambda e, qbk=qbk: e.matmul(pb[qbk][:], ones1[:], qrow[:],
                                             start=True, stop=True),
                 reads=["qrow"], writes=[f"pb{qbk}"], inc=("pe", 1),
                 tag=f"mmbq_p{p}t{tb2}")
            B.op("vector",
                 lambda e, mbk=mbk: e.tensor_scalar_mul(lnA, pb[mbk][:],
                                                        1.0 / D),
                 reads=[f"pb{mbk}"], writes=["g0"], tag=f"mb_p{p}t{tb2}")
            B.op("scalar", lambda e: e.square(lnB, lnA),
                 reads=["g0"], writes=["g1"], tag=f"m2b_p{p}t{tb2}")
            B.op("vector",
                 lambda e, qbk=qbk: e.tensor_scalar(
                     lnC, pb[qbk][:], 1.0 / D, EPS_LN, ALU.mult, ALU.add),
                 reads=[f"pb{qbk}"], writes=["sg0"], tag=f"qe_p{p}t{tb2}")
            B.op("vector",
                 lambda e: e.tensor_sub(lnD, lnC, lnB),
                 reads=["sg0", "g1"], writes=["sg1"], tag=f"var_p{p}t{tb2}")
            B.op("scalar",
                 lambda e: e.activation(lnB, lnD, AF.Sqrt),
                 reads=["sg1"], writes=["g1"], tag=f"sd_p{p}t{tb2}")
            B.op("vector", lambda e: e.reciprocal(lnC, lnB),
                 reads=["g1"], writes=["sg0"], tag=f"rr_p{p}t{tb2}")
            B.op("vector", lambda e: e.tensor_mul(lnB, lnA, lnC),
                 reads=["g0", "sg0"], writes=["g1"], tag=f"mrr_p{p}t{tb2}")
            B.op("scalar", lambda e, tb2=tb2: e.copy(rb_v[tb2], lnC),
                 reads=["sg0"], writes=["sq0"], tag=f"rbf_p{p}t{tb2}")
            B.op("scalar", lambda e, tb2=tb2: e.copy(mrb_v[tb2], lnB),
                 reads=["g1"], writes=["sq1"], tag=f"mrbf_p{p}t{tb2}")
        for kc in range(NK):
            xsl = xn[:, kc * TP:(kc + 1) * TP]
            B.op("vector",
                 lambda e, xsl=xsl: e.tensor_mul(txn[:], xsl, rb_all),
                 reads=[f"xn{kc}", "sq0"], writes=["txn"],
                 tag=f"n1_p{p}k{kc}")
            B.op("vector",
                 lambda e, xsl=xsl: e.tensor_sub(xsl, txn[:], mrb_all),
                 reads=["txn", "sq1"], writes=[f"xn{kc}"],
                 tag=f"n2_p{p}k{kc}")

    def plan_p2(p):
        for tb2 in range(2):
            zfns, zrd, lfns, lrd = [], [], [], []
            for kc in range(NK):
                def zf(e, kc=kc, tb2=tb2):
                    return e.matmul(
                        pb[tb2][:], lA[:, kc * 128:(kc + 1) * 128],
                        xn[:, kc * TP + tb2 * TB: kc * TP + (tb2 + 1) * TB],
                        start=(kc == 0), stop=(kc == NK - 1))
                zfns.append(zf)
                zrd.append(["lA", f"xn{kc}"])

                def lf(e, kc=kc, tb2=tb2):
                    return e.matmul(
                        pb[2 + tb2][0:16, :], lR[:, kc * 16:(kc + 1) * 16],
                        xn[:, kc * TP + tb2 * TB: kc * TP + (tb2 + 1) * TB],
                        start=(kc == 0), stop=(kc == NK - 1))
                lfns.append(lf)
                lrd.append(["lR", f"xn{kc}"])
            B.acc_group("tensor", zfns, zrd, [f"pb{tb2}"], inc=("pe", 1),
                        tag=f"mmz_p{p}t{tb2}")
            B.acc_group("tensor", lfns, lrd, [f"pb{2 + tb2}"], inc=("pe", 1),
                        tag=f"mmlg_p{p}t{tb2}")
            B.op("vector",
                 lambda e, tb2=tb2: e.tensor_copy(
                     z_sb[:, tb2 * TB:(tb2 + 1) * TB], pb[tb2][:]),
                 reads=[f"pb{tb2}"], writes=["z_sb"], tag=f"evz_p{p}t{tb2}")


    def plan_p2_alpha(p, tb2):
        bk = [(4, 5, 6, 7), (0, 1, 2, 3)][tb2]
        srw = srow if tb2 == 0 else qrow
        srn = "srow" if tb2 == 0 else "qrow"
        B.op("vector",
             lambda e, tb2=tb2: e.tensor_copy(lg_tmp2[tb2][:],
                                              pb[2 + tb2][0:16, :]),
             reads=[f"pb{2 + tb2}"], writes=["lg"], tag=f"evlg_p{p}t{tb2}")
        B.op("tensor",
             lambda e, tb2=tb2, bk=bk: e.matmul(pb[bk[0]][0:4, :], st_sb[:],
                                                lg_tmp2[tb2][:],
                                                start=True, stop=True),
             reads=["st", "lg"], writes=[f"pb{bk[0]}"], inc=("pe", 1),
             tag=f"mmlsel_p{p}t{tb2}")
        B.op("scalar",
             lambda e, tb2=tb2, bk=bk: e.activation(esb2[tb2][:],
                                                    pb[bk[0]][0:4, :], AF.Exp,
                                                    bias=ce_sb[:], scale=1.0),
             reads=[f"pb{bk[0]}", "ce"], writes=[f"esb{tb2}"],
             tag=f"exp_p{p}t{tb2}")
        B.op("tensor",
             lambda e, tb2=tb2, bk=bk: e.matmul(pb[bk[1]][0:1, :], ones4[:],
                                                esb2[tb2][:],
                                                start=True, stop=True),
             reads=[f"esb{tb2}"], writes=[f"pb{bk[1]}"], inc=("pe", 1),
             tag=f"mmss_p{p}t{tb2}")
        B.op("vector",
             lambda e, bk=bk, srw=srw: e.tensor_copy(srw[0:1, :],
                                                     pb[bk[1]][0:1, :]),
             reads=[f"pb{bk[1]}"], writes=[srn], tag=f"evss_p{p}t{tb2}")
        B.op("vector", lambda e, srw=srw: e.reciprocal(srw[0:1, :], srw[0:1, :]),
             reads=[srn], writes=[srn], tag=f"rs_p{p}t{tb2}")
        B.op("tensor",
             lambda e, bk=bk, srw=srw: e.matmul(pb[bk[2]][:], ones1[:],
                                                srw[0:1, :],
                                                start=True, stop=True),
             reads=[srn], writes=[f"pb{bk[2]}"], inc=("pe", 1),
             tag=f"mmrsb_p{p}t{tb2}")
        B.op("vector",
             lambda e, tb2=tb2, bk=bk: e.tensor_copy(rs_bf[tb2][:],
                                                     pb[bk[2]][:]),
             reads=[f"pb{bk[2]}"], writes=[f"rsbf{tb2}"],
             tag=f"evrs_p{p}t{tb2}")
        B.op("tensor",
             lambda e, tb2=tb2, bk=bk: e.matmul(pb[bk[3]][:], m2f[:],
                                                esb2[tb2][:],
                                                start=True, stop=True),
             reads=["m2f", f"esb{tb2}"], writes=[f"pb{bk[3]}"], inc=("pe", 1),
             tag=f"mmau_p{p}t{tb2}")
        B.op("vector",
             lambda e, tb2=tb2, bk=bk: e.tensor_scalar_mul(aexp2[tb2][:],
                                                           pb[bk[3]][:],
                                                           SCALING),
             reads=[f"pb{bk[3]}"], writes=[f"aexp{tb2}"],
             tag=f"au2_p{p}t{tb2}")
        txv = txn[:, tb2 * TB:(tb2 + 1) * TB]
        B.op("vector",
             lambda e, tb2=tb2, txv=txv: e.tensor_mul(
                 txv, z_sb[:, tb2 * TB:(tb2 + 1) * TB], aexp2[tb2][:]),
             reads=["z_sb", f"aexp{tb2}"], writes=["txn"],
             tag=f"zw1_p{p}t{tb2}")
        B.op("vector",
             lambda e, tb2=tb2, txv=txv: e.tensor_mul(
                 zw[:, tb2 * TB:(tb2 + 1) * TB], txv, rs_bf[tb2][:]),
             reads=["txn", f"rsbf{tb2}"], writes=[f"zw{tb2}"],
             tag=f"zw2_p{p}t{tb2}")

    def plan_iter(p, i):
        par = i % 2
        base = par * 4
        mini_bank = (1 - par) * 4 + 3
        banding = (p == 0)

        if i + 1 < NN:
            sl = (i + 1) % 2
            dma(f"sx{sl}", xbuf[sl][:],
                d_xt[(i + 1) * 128:(i + 2) * 128, p * TP:(p + 1) * TP],
                writes=[f"xb{sl}"], tag=f"xr_p{p}i{i + 1}")
            w_loads_half(p, i + 1, 0)

        bj = i + 11
        if banding and i in (5, 13, 21):
            bounce_in_g((i + 10) // 8)
        if banding and bj <= 31:
            band_mask_half(bj, 0)

        for tb in range(2):
            sbank = base + tb
            ybank = base + 2 + tb
            fns, rds = [], []
            for kc in range(NK):
                def f(e, kc=kc, tb=tb, sbank=sbank, i=i):
                    return e.matmul(
                        pb[sbank][:], wm[i % 2][:, kc * 128:(kc + 1) * 128],
                        xn[:, kc * TP + tb * TB: kc * TP + (tb + 1) * TB],
                        start=(kc == 0), stop=(kc == NK - 1))
                fns.append(f)
                rds.append([f"wm{i % 2}", f"xn{kc}"])
            B.acc_group("tensor", fns, rds, [f"pb{sbank}"], inc=("pe", 1),
                        tag=f"mms_p{p}i{i}t{tb}")

            if tb == 0:
                if i + 1 < NN:
                    wm_mults_half(p, i + 1, 0)
                    wb_casts_half(p, i + 1, 0)
                    w_loads_half(p, i + 1, 1)
                if banding and bj <= 31:
                    band_mask_half(bj, 1)

            fns, rds = [], []
            for kc in range(NK):
                def f(e, kc=kc, tb=tb, ybank=ybank, i=i):
                    return e.matmul(
                        pb[ybank][:], wb[i % 2][:, kc * 128:(kc + 1) * 128],
                        xn[:, kc * TP + tb * TB: kc * TP + (tb + 1) * TB],
                        start=(kc == 0), stop=False)
                fns.append(f)
                rds.append([f"wb{i % 2}", f"xn{kc}"])

            def fB(e, tb=tb, ybank=ybank, i=i):
                return e.matmul(pb[ybank][:], lB[:, i * 128:(i + 1) * 128],
                                zw[:, tb * TB:(tb + 1) * TB],
                                start=False, stop=True)
            fns.append(fB)
            rds.append(["lB", f"zw{tb}"])
            B.acc_group("tensor", fns, rds, [f"pb{ybank}"], inc=("pe", 1),
                        tag=f"mmy_p{p}i{i}t{tb}")

            if tb == 1 and i + 1 < NN:
                wm_mults_half(p, i + 1, 1)
                wb_casts_half(p, i + 1, 1)
            if tb == 1 and banding and bj <= 31:
                band_pooledmm(bj, mini_bank)
                band_pevac(bj, mini_bank)

            B.op("scalar",
                 lambda e, tb=tb, ybank=ybank, i=i: e.activation(
                     g_sb[:, tb * TB:(tb + 1) * TB], pb[ybank][:], AF.Silu,
                     bias=b_sb[:, i:i + 1], scale=1.0),
                 reads=[f"pb{ybank}", "b_sb"], writes=[f"g{tb}"],
                 tag=f"silu_p{p}i{i}t{tb}")
            B.op("vector",
                 lambda e, tb=tb, sbank=sbank: e.tensor_mul(
                     sg_sb[:, tb * TB:(tb + 1) * TB], pb[sbank][:],
                     g_sb[:, tb * TB:(tb + 1) * TB]),
                 reads=[f"pb{sbank}", f"g{tb}"], writes=[f"sg{tb}"],
                 tag=f"sg_p{p}i{i}t{tb}")
            B.op("gpsimd",
                 lambda e, tb=tb, i=i: e.tensor_tensor(
                     out_sb[i % 2][:, tb * TB:(tb + 1) * TB],
                     sg_sb[:, tb * TB:(tb + 1) * TB],
                     xbuf[i % 2][:, tb * TB:(tb + 1) * TB], ALU.add),
                 reads=[f"sg{tb}", f"xb{i % 2}"], writes=[f"o{i % 2}"],
                 tag=f"res_p{p}i{i}t{tb}")

        if banding and i in (5, 13, 21):
            g = (i + 10) // 8
            opp = (1 - par) * 4
            for k, bb in enumerate(range(g * 8, g * 8 + 8)):
                bank = opp + 2 + (k % 2)
                band_e4l(bb, bank)
                band_texp_evac(bb, bank)
        gdma(f"so{i % 2}", d_out[i * 128:(i + 1) * 128, p * TP:(p + 1) * TP],
             out_sb[i % 2][:], reads=[f"o{i % 2}"], tag=f"st_p{p}i{i}")
        if banding and bj == 31 or banding and bj in (15, 23):
            bounce_out_g(bj // 8)

    plan_startup()
    for p in range(NPASS):
        plan_ln(p)
        plan_p2(p)
        w_prep_serial(p, 0)
        plan_p2_alpha(p, 0)
        plan_p2_alpha(p, 1)
        dma("sx0", xbuf[0][:], d_xt[0:128, p * TP:(p + 1) * TP],
            writes=["xb0"], tag=f"xr_p{p}i0")
        for i in range(NN):
            plan_iter(p, i)

    B.finalize()
    fin = []
    for s in ("so0", "so1", "sbb0", "sbb1", "sm0", "sm1"):
        if s in B.counts:
            fin.append(B.ev_at(s, B.counts[s]))
    B.ops["gpsimd"].append(Op("gpsimd", "WAIT_ONLY", fin, None, "final_wait"))
    B.validate()

    semh = {s: E(nc.semaphore(s)) for s in sorted(B.semnames)}
    blk = E(nc.Block())

    def emitter(name):
        def run(eng):
            hw = {}
            own = B.ENG_SEM.get(name)
            for o in B.ops[name]:
                for w in o.waits:
                    if hw.get(w.sem, -1) >= w.count:
                        continue
                    eng.wait_ge(semh[w.sem], w.count)
                    hw[w.sem] = w.count
                if o.fn == "NOP_INC":
                    if name == "tensor":
                        ins = eng.matmul(pb[0][0:1, 0:1], ones1[0:1, 0:1],
                                         ones1[0:1, 0:1], start=True, stop=True)
                    elif name == "gpsimd":
                        ins = eng.memset(sgn_t[0:1, 0:1], 0.0)
                    elif name == "scalar":
                        ins = eng.copy(sgn_t[0:1, 0:1], sgn_t[0:1, 0:1])
                    else:
                        ins = eng.tensor_copy(sgn_t[0:1, 0:1], sgn_t[0:1, 0:1])
                    ins.then_inc(semh[o.inc[0]], o.inc[1])
                    continue
                if o.fn == "WAIT_ONLY":
                    continue
                ins = o.fn(eng)
                if o.inc is not None:
                    ins.then_inc(semh[o.inc[0]], o.inc[1])
        return run

    blk.sync(emitter("sync"))
    blk.tensor(emitter("tensor"))
    blk.vector(emitter("vector"))
    blk.scalar(emitter("scalar"))
    blk.gpsimd(emitter("gpsimd"))
    ctx.close()
    return nc


def kernel(L0_attn_data, L0_attn_weight, L0_attn_mask, L0_ffn_W_base,
           L0_ffn_b_base, L0_ffn_lora_A, L0_ffn_lora_B, L0_ffn_router_W,
           L0_ffn_router_b, L0_ffn_domain_prior, L0_ffn_active):
    x = np.ascontiguousarray(np.asarray(L0_attn_data, np.float32)
                             .reshape(16384, D))
    W = np.asarray(L0_attn_weight, np.float32)
    mask = np.asarray(L0_attn_mask, np.int32)
    Wb = np.asarray(L0_ffn_W_base, np.float32)
    bbase = np.asarray(L0_ffn_b_base, np.float32)
    lora_A = np.asarray(L0_ffn_lora_A, np.float32)
    lora_B = np.asarray(L0_ffn_lora_B, np.float32)
    rW = np.asarray(L0_ffn_router_W, np.float32)
    rb = np.asarray(L0_ffn_router_b, np.float32)
    prior = np.asarray(L0_ffn_domain_prior, np.float32)
    active = np.asarray(L0_ffn_active, np.int32)

    def prep2(M):
        return np.ascontiguousarray(
            M.reshape(NN, 128, NK, 128).transpose(0, 3, 2, 1)
            .reshape(NN, 128, D))

    wp = prep2(W)
    wbp = prep2(Wb)
    mkp = np.ascontiguousarray(
        mask.reshape(D, 4, 32, 32).transpose(0, 1, 3, 2).reshape(D, D))
    ap_ = np.ascontiguousarray(
        lora_A.transpose(2, 0, 1).reshape(D, 128).reshape(NK, 128, 128)
        .transpose(1, 0, 2).reshape(128, D))
    rp = np.ascontiguousarray(
        rW.T.reshape(NK, 128, 16).transpose(1, 0, 2).reshape(128, 512))
    bp = np.ascontiguousarray(lora_B.transpose(0, 2, 1).reshape(128, D))

    e4l = np.zeros((4, 128), BF)
    for j in range(4):
        e4l[j, j * 32:(j + 1) * 32] = 1
    e4p = np.ascontiguousarray(e4l.T)
    ekron = np.zeros((16, 128), np.float32)
    for Ei in range(16):
        ekron[Ei, Ei * 8:(Ei + 1) * 8] = 1

    common = dict(
        wp=wp, wbp=wbp, mkp=mkp, ap_=ap_, rp=rp, bp=bp,
        bb=np.ascontiguousarray(bbase.reshape(32, 128)),
        rbv=np.ascontiguousarray(rb.reshape(16, 1)),
        pri=np.ascontiguousarray(prior.reshape(16, 1)),
        actv=np.ascontiguousarray(active.reshape(1, 4)),
        ident=np.eye(32, dtype=np.float32),
        ones1f=np.ones((1, 128), np.float32),
        onesbf=np.ones((128, 1), BF),
        e4l=e4l, e4p=e4p, ekron=ekron,
        iota16=np.arange(16, dtype=np.float32).reshape(1, 16),
        ones4f=np.ones((4, 1), np.float32),
    )
    in_maps = []
    for c in range(NCORE):
        m = dict(common)
        m["xt"] = np.ascontiguousarray(x[c * TPC:(c + 1) * TPC].T)
        in_maps.append(m)

    nc = build_program()
    res = run_bass_kernel_spmd(nc, in_maps, core_ids=list(range(NCORE)))
    out = np.concatenate([np.asarray(r["out"]) for r in res.results], axis=1)
    return np.ascontiguousarray(out.T).reshape(2, 8192, D).astype(np.float32)


if __name__ == "__main__":
    build_program()
    print("build ok")
